# version 40
# baseline (speedup 1.0000x reference)
import numpy as np
import ml_dtypes
from contextlib import ExitStack

import concourse.bass as bass
import concourse.mybir as mybir
from concourse.bass_utils import run_bass_kernel_spmd

F32 = mybir.dt.float32
BF16 = mybir.dt.bfloat16
AF = mybir.ActivationFunctionType
ALU = mybir.AluOpType
AX = mybir.AxisListType

T = 4096
C = 1024
H = 16
DH = 64
HP = 8
NT = 32
CH = 8
E = 8
GAMMAS = [1.0 - 2.0 ** (-5.0 - h) for h in range(H)]
WB = 8320
WF = 226


def _host_consts():
    f32 = np.float32
    bf = ml_dtypes.bfloat16
    inv_freq = (10000.0 ** (-(np.arange(0, DH, 2, dtype=np.float64) / DH)))
    tglob = np.arange(T, dtype=np.float64)
    ang = tglob[:, None] * inv_freq[None, :]
    emb = np.concatenate([ang, ang], axis=1)
    cos = np.cos(emb)
    sin = np.sin(emb)
    ssin = sin.copy()
    ssin[:, :DH // 2] *= -1.0
    cosT = np.zeros((128, NT * DH), dtype=bf)
    ssinT = np.zeros((128, NT * DH), dtype=bf)
    for i in range(NT):
        cosT[:, i * DH:(i + 1) * DH] = cos[i * 128:(i + 1) * 128, :].astype(bf)
        ssinT[:, i * DH:(i + 1) * DH] = ssin[i * 128:(i + 1) * 128, :].astype(bf)
    p = np.arange(128, dtype=np.float64)
    tqf = np.zeros((128, C), dtype=bf)
    tkf = np.zeros((128, C), dtype=bf)
    for h in range(H):
        g = GAMMAS[h]
        tqf[:, h * DH:(h + 1) * DH] = ((g ** p) / 8.0).astype(bf)[:, None]
        tkf[:, h * DH:(h + 1) * DH] = (g ** (128.0 - p)).astype(bf)[:, None]
    g1t = np.zeros((128, H * 128), dtype=bf)
    si = np.arange(128, dtype=np.float64)
    for h in range(H):
        g = GAMMAS[h]
        m = (np.arange(128)[None, :] >= si[:, None]).astype(np.float64)
        g1t[:, h * 128:(h + 1) * 128] = ((g ** (-si))[:, None] * m).astype(bf)
    gp = np.zeros((128, HP), dtype=f32)
    for hp in range(HP):
        gp[0:64, hp] = GAMMAS[2 * hp] ** 128.0
        gp[64:128, hp] = GAMMAS[2 * hp + 1] ** 128.0
    cvec = np.broadcast_to((np.arange(E) + 1000.0).astype(f32), (128, E)).copy()
    epsv = np.zeros((128, 2), dtype=f32)
    epsv[:, 0] = 1e-5
    epsv[:, 1] = -0.5
    identf = np.eye(128, dtype=f32)
    identb = np.eye(128, dtype=bf)
    sel2 = np.zeros((2, 128), dtype=bf)
    sel2[0, 0:64] = 1
    sel2[1, 64:128] = 1
    cb16 = np.concatenate([tqf, tkf, identb, cosT, ssinT, g1t], axis=1)
    return dict(cb16=np.ascontiguousarray(cb16), gp=gp, cvec=cvec, epsv=epsv,
                identf=identf, sel2=sel2)


class Tok:
    __slots__ = ("incs",)

    def __init__(self):
        self.incs = []

    def then_inc(self, h, k):
        self.incs.append((h, k))
        return self


class REng:
    def __init__(self, ops, name):
        object.__setattr__(self, "_ops", ops)
        object.__setattr__(self, "_wname", name)

    def wait_ge(self, h, v):
        self._ops.append(("w", h, v, None))

    def __getattr__(self, name):
        ops = self._ops

        def call(*a, **kw):
            tok = Tok()
            ops.append(("o", name, (a, kw), tok))
            return tok
        return call


class Rec:
    ENGINES = ("sync", "gpsimd", "scalar", "vector", "tensor")

    def __init__(self, block):
        self.block = block
        self.ops = {e: [] for e in self.ENGINES}

    def eng(self, name):
        return REng(self.ops[name], name)

    @property
    def sync(self):
        return self.eng("sync")

    @property
    def gpsimd(self):
        return self.eng("gpsimd")

    @property
    def scalar(self):
        return self.eng("scalar")

    @property
    def vector(self):
        return self.eng("vector")

    @property
    def tensor(self):
        return self.eng("tensor")

    def flush(self):
        for e in self.ENGINES:
            ops = self.ops[e]
            if not ops:
                continue

            def replay(eng, ops=list(ops)):
                for kind, x, y, tok in ops:
                    if kind == "w":
                        eng.wait_ge(x, y)
                    else:
                        a, kw = y
                        instr = getattr(eng, x)(*a, **kw)
                        for (h, k) in tok.incs:
                            instr.then_inc(h, k)
            getattr(self.block, e)(replay)
            ops.clear()


class Sem:
    def __init__(self, stack, nc, name):
        self.h = stack.enter_context(nc.semaphore(name))
        self.n = 0

    def inc(self, tok, k=1):
        tok.then_inc(self.h, k)
        self.n += k
        return self.n


_WMAX = {}


def _w(eng, sem, n=None):
    n = sem.n if n is None else n
    if n <= 0:
        return
    ename = getattr(eng, "_wname", None) or type(eng).__name__
    key = (ename, id(sem))
    if _WMAX.get(key, -1) >= n:
        return
    _WMAX[key] = n
    eng.wait_ge(sem.h, n)


STOP = "ALL"
SUB = 9
EVAC = "both"
_ORD = {"NONE": -1, "PRE": 0, "MID": 1, "RET": 2, "BCK1": 3, "BCK2": 4, "A": 5, "ALL": 6}


def _on(stage):
    return _ORD[stage] <= _ORD[STOP]


def build_core(nt=NT):
    _WMAX.clear()
    nc = bass.Bass("TRN2", target_bir_lowering=False)

    x = nc.dram_tensor("x", [T, C], F32, kind="ExternalInput").ap()
    wqT = nc.dram_tensor("wqT", [C, C], F32, kind="ExternalInput").ap()
    wkT = nc.dram_tensor("wkT", [C, C], F32, kind="ExternalInput").ap()
    wvT = nc.dram_tensor("wvT", [C, C], F32, kind="ExternalInput").ap()
    woT = nc.dram_tensor("woT", [C, C], F32, kind="ExternalInput").ap()
    ewT = nc.dram_tensor("ewT", [E, C, C], F32, kind="ExternalInput")
    cb16_d = nc.dram_tensor("cb16", [128, WB], BF16, kind="ExternalInput").ap()
    cf32_d = nc.dram_tensor("cf32", [128, WF], F32, kind="ExternalInput").ap()
    sel2_d = nc.dram_tensor("sel2", [2, 128], BF16, kind="ExternalInput").ap()

    out = nc.dram_tensor("out", [T, C], F32, kind="ExternalOutput").ap()
    xnd = nc.dram_tensor("xnd", [T, C], BF16).ap()

    st = ExitStack()
    with st:
        dx = Sem(st, nc, "dx")
        dx2 = Sem(st, nc, "dx2")
        dw = Sem(st, nc, "dw")
        dwg = Sem(st, nc, "dwg")
        dwk = Sem(st, nc, "dwk")
        dwv = Sem(st, nc, "dwv")
        dwo = Sem(st, nc, "dwo")
        dxn = Sem(st, nc, "dxn")
        dop = Sem(st, nc, "dop")
        dwe = Sem(st, nc, "dwe")
        dxe = Sem(st, nc, "dxe")
        pe = Sem(st, nc, "pe")
        dv = Sem(st, nc, "dv")
        ac = Sem(st, nc, "ac")
        pl = Sem(st, nc, "pl")

        sMask = st.enter_context(nc.sbuf_tensor("sMask", [128, NT * E], F32))

        B = [st.enter_context(nc.psum_tensor(f"ps{i}", [128, 512], F32))
             for i in range(6)]
        PBQ = st.enter_context(nc.psum_tensor("pbq", [128, 1024], BF16))
        PBK = st.enter_context(nc.psum_tensor("pbk", [128, 1024], BF16))

        block = st.enter_context(nc.Block())

        sWq = st.enter_context(nc.sbuf_tensor("sWq", [128, CH * C], BF16))
        sWk = st.enter_context(nc.sbuf_tensor("sWk", [128, CH * C], BF16))
        sWv = st.enter_context(nc.sbuf_tensor("sWv", [128, CH * C], BF16))

        stA = ExitStack()
        with stA:
            A = stA.enter_context
            sWo = A(nc.sbuf_tensor("sWo", [128, CH * C], BF16))
            sGnbB = A(nc.sbuf_tensor("sGnbB", [128, CH], BF16))
            sBeta = A(nc.sbuf_tensor("sBeta", [1, C], BF16))
            sOnes1 = A(nc.sbuf_tensor("sOnes1", [1, 128], BF16))
            sSel2 = A(nc.sbuf_tensor("sSel2", [2, 128], BF16))
            sBlk2 = A(nc.sbuf_tensor("sBlk2", [128, 2], BF16))
            sCB = A(nc.sbuf_tensor("sCB", [128, WB], BF16))
            sCF = A(nc.sbuf_tensor("sCF", [128, WF], F32))
            sX = A(nc.sbuf_tensor("sX", [128, 2 * C], F32))
            sX2 = A(nc.sbuf_tensor("sX2", [128, 2 * C], F32))
            sLn = A(nc.sbuf_tensor("sLn", [128, 10 * 12], F32))
            sXn = A(nc.sbuf_tensor("sXn", [128, 2 * C], F32))
            sXnF = A(nc.sbuf_tensor("sXnF", [128, 2 * C], F32))
            sXnB = A(nc.sbuf_tensor("sXnB", [128, 3 * C], BF16))
            sDump = A(nc.sbuf_tensor("sDump", [128, C], BF16))
            sSc = A(nc.sbuf_tensor("sSc", [8, 128], F32))
            sGsc = A(nc.sbuf_tensor("sGsc", [128, 32], F32))
            sQ = A(nc.sbuf_tensor("sQ", [128, 2 * C], BF16))
            sK = A(nc.sbuf_tensor("sK", [128, 2 * C], BF16))
            sRc = A(nc.sbuf_tensor("sRc", [128, C], BF16))
            sQt = A(nc.sbuf_tensor("sQt", [128, 2 * C], BF16))
            sKp = A(nc.sbuf_tensor("sKp", [128, 2 * C], BF16))
            sKtB = A(nc.sbuf_tensor("sKtB", [128, 3 * C], BF16))
            sV = A(nc.sbuf_tensor("sV", [128, 4 * C], BF16))
            sQf = A(nc.sbuf_tensor("sQf", [128, 2 * C], BF16))
            sKf = A(nc.sbuf_tensor("sKf", [128, 2 * C], BF16))
            sW1 = A(nc.sbuf_tensor("sW1", [128, 2 * 256], BF16))
            sS = A(nc.sbuf_tensor("sS", [128, HP * DH], F32))
            sSb = A(nc.sbuf_tensor("sSb", [128, HP * DH], BF16))
            sAv = A(nc.sbuf_tensor("sAv", [128, 2 * C], F32))
            sAvB = A(nc.sbuf_tensor("sAvB", [128, C], BF16))
            sAv2 = A(nc.sbuf_tensor("sAv2", [128, C], BF16))
            sSt = A(nc.sbuf_tensor("sSt", [2, 2048], F32))
            sStB = A(nc.sbuf_tensor("sStB", [2, 2048], BF16))
            sGn = A(nc.sbuf_tensor("sGn", [128, C], BF16))
            sOutp = A(nc.sbuf_tensor("sOutp", [128, 2 * C], F32))

            def _pre_sc(scalar):
                dw.inc(scalar.dma_start(out=sCF[:], in_=cf32_d), 16)
                dw.inc(scalar.dma_start(out=sCB[:], in_=cb16_d), 16)
                dw.inc(scalar.dma_start(out=sSel2[:], in_=sel2_d), 16)
            block.scalar(_pre_sc)

            def _pre_gp(gpsimd):
                dx.inc(gpsimd.dma_start(out=sX[:, 0:C], in_=x[0:128, :]), 16)
                dwg.inc(gpsimd.dma_start(
                    out=sWq[:].rearrange("p (j o) -> p j o", j=CH),
                    in_=wqT.rearrange("(j p) o -> p j o", p=128)), 16)
                dwk.inc(gpsimd.dma_start(
                    out=sWk[:].rearrange("p (j o) -> p j o", j=CH),
                    in_=wkT.rearrange("(j p) o -> p j o", p=128)), 16)
                dwv.inc(gpsimd.dma_start(
                    out=sWv[:].rearrange("p (j o) -> p j o", j=CH),
                    in_=wvT.rearrange("(j p) o -> p j o", p=128)), 16)
                dwo.inc(gpsimd.dma_start(
                    out=sWo[:].rearrange("p (j o) -> p j o", j=CH),
                    in_=woT.rearrange("(j p) o -> p j o", p=128)), 16)
                pl.inc(gpsimd.memset(sS[:], 0.0))
                pl.inc(gpsimd.memset(sSb[:], 0.0))
                pl.inc(gpsimd.memset(sOnes1[:], 1.0))
                gpsimd.memset(sBlk2[:], 0.0)
                gpsimd.memset(sBlk2[0:64, 0:1], 1.0 / 64.0)
                pl.inc(gpsimd.memset(sBlk2[64:128, 1:2], 1.0 / 64.0))
            block.gpsimd(_pre_gp)

            rec = Rec(block)
            guard = {}
            M = {}

            def need(eng, key):
                if key in guard:
                    s, n = guard[key]
                    _w(eng, s, n)

            pre_we = {}
            pre_xn = {}
            for it in range(nt + 9):
                tLN, tLN2, tXT, tGP = it, it - 1, it - 2, it - 3
                tRP, tRT, tRE, tG1, tG2 = it - 4, it - 5, it - 6, it - 7, it - 8

                def ok(t):
                    return 0 <= t < nt

                sy = rec.sync
                if tLN == 0:
                    M[("dx", 0)] = 16
                elif ok(tLN):
                    need(sy, ("sX", tLN % 2))
                    _w(sy, dx, dx.n)
                    dx.inc(sy.dma_start(
                        out=sX[:, (tLN % 2) * C:(tLN % 2) * C + C],
                        in_=x[tLN * 128:(tLN + 1) * 128, :]), 16)
                    M[("dx", tLN)] = dx.n
                if ok(tG1):
                    need(sy, ("sX2", tG1 % 2))
                    _w(sy, dx2, dx2.n)
                    dx2.inc(sy.dma_start(
                        out=sX2[:, (tG1 % 2) * C:(tG1 % 2) * C + C],
                        in_=x[tG1 * 128:(tG1 + 1) * 128, :]), 16)
                    M[("dx2", tG1)] = dx2.n
                if ok(tGP):
                    _w(sy, dv, M[("xnb", tGP)])
                    _w(sy, ac, M[("xnf", tGP)])
                    _w(sy, dxn, dxn.n)
                    dxn.inc(sy.dma_start(
                        out=xnd[tGP * 128:(tGP + 1) * 128, :],
                        in_=sXnB[:, (tGP % 3) * C:(tGP % 3) * C + C]), 16)
                    M[("dxn", tGP)] = dxn.n

                if ok(tXT):
                    t_ = tXT
                    xn_ = (t_ % 2) * C
                    te = rec.tensor
                    if t_ == 0:
                        _w(te, dw, 16)
                    _w(te, dv, M[("xn", t_)])
                    lq8 = min(it - 8, nt - 1)
                    if lq8 >= 0:
                        _w(te, dv, M[("w1", lq8, HP - 1)])
                    lo9 = min(it - 9, nt - 1)
                    if lo9 >= 0:
                        _w(te, dv, M[("outp", lo9)])
                    if t_ >= 1:
                        _w(te, ac, M[("xnf", t_ - 1)])
                        _w(te, dv, M[("xnb", t_ - 1)])
                    last = None
                    for j in range(CH):
                        b = B[j // 4]
                        last = te.transpose(b[:, (j % 4) * 128:(j % 4) * 128 + 128],
                                            sXn[:, xn_ + j * 128:xn_ + j * 128 + 128],
                                            sCF[:, 0:128])
                    pe.inc(last)
                    M[("xt", t_)] = pe.n
                    guard[("sXn", t_ % 2)] = (pe, pe.n)
                    sc3 = rec.scalar
                    xnf_ = (t_ % 2) * C
                    xnb_ = (t_ % 3) * C
                    _w(sc3, pe, M[("xt", t_)])
                    need(sc3, ("sXnF", t_ % 2))
                    need(sc3, ("sXnB", t_ % 3))
                    if t_ >= 3:
                        _w(sc3, dxn, M[("dxn", t_ - 3)])
                    sc3.copy(sXnF[:, xnf_:xnf_ + 512], B[0][:, 0:512])
                    ac.inc(sc3.copy(sXnB[:, xnb_:xnb_ + 512], B[0][:, 0:512]))
                    M[("xnf", t_)] = ac.n
                    ve3 = rec.vector
                    _w(ve3, pe, M[("xt", t_)])
                    need(ve3, ("sXnF", t_ % 2))
                    need(ve3, ("sXnB", t_ % 3))
                    if t_ >= 3:
                        _w(ve3, dxn, M[("dxn", t_ - 3)])
                    ve3.tensor_copy(sXnF[:, xnf_ + 512:xnf_ + 1024], B[1][:, 0:512])
                    dv.inc(ve3.tensor_copy(sXnB[:, xnb_ + 512:xnb_ + 1024],
                                           B[1][:, 0:512]))
                    M[("xnb", t_)] = dv.n

                if ok(tG2):
                    t_ = tG2
                    av_ = (t_ % 2) * C
                    te = rec.tensor
                    _w(te, dv, M[("stb", t_)])
                    if t_ == 0:
                        _w(te, dw, 48)
                    lr7 = min(it - 7, nt - 1)
                    if lr7 >= 0:
                        _w(te, dv, M[("w1", lr7, HP - 1)])
                    if ok(tG1) or tG1 >= nt:
                        lr = min(tG1, nt - 1)
                        _w(te, ac, M[("avev", lr)])
                        _w(te, dv, M[("supd", lr, HP - 1)])
                    _w(te, ac, M[("ste2", t_)])
                    te.matmul(B[2][:], sSel2[:], sStB[0:2, 0:512],
                              start=True, stop=True)
                    te.matmul(B[4][:], sSel2[:], sStB[0:2, 1024:1536],
                              start=True, stop=True)
                    te.matmul(B[3][:], sSel2[:], sStB[0:2, 512:1024],
                              start=True, stop=True)
                    pe.inc(te.matmul(B[5][:], sSel2[:], sStB[0:2, 1536:2048],
                                     start=True, stop=True))
                    M[("bcast", t_)] = pe.n
                    guard[("sStB", 0)] = (pe, pe.n)
                    ve = rec.vector
                    _w(ve, pe, M[("bcast", t_)])
                    need(ve, ("sGn", 0))
                    dv.inc(ve.tensor_sub(sAv[:, av_:av_ + 512], sAv[:, av_:av_ + 512],
                                         B[2][:]))
                    _w(ve, dv)
                    dv.inc(ve.tensor_tensor(sGn[:, 0:512], sAv[:, av_:av_ + 512],
                                            B[4][:], ALU.mult))
                    M[("gnh0", t_)] = dv.n
                    _w(ve, dv)
                    dv.inc(ve.tensor_sub(sAv[:, av_ + 512:av_ + 1024],
                                         sAv[:, av_ + 512:av_ + 1024], B[3][:]))
                    _w(ve, dv)
                    dv.inc(ve.tensor_tensor(sGn[:, 512:1024],
                                            sAv[:, av_ + 512:av_ + 1024],
                                            B[5][:], ALU.mult))
                    M[("gn", t_)] = dv.n
                    guard[("sAv", t_ % 2)] = (dv, dv.n)
                if ok(tRT):
                    t_ = tRT
                    qt_ = (t_ % 2) * C
                    te = rec.tensor
                    if t_ == 0:
                        _w(te, dw, 32)
                    _w(te, pl, M[("ropeq", t_)])
                    if t_ >= 1:
                        _w(te, ac, M[("qfev", t_ - 1)])
                        _w(te, dv, M[("kfev", t_ - 1)])
                    kp_ = (t_ % 2) * C
                    last = None
                    for hp in range(HP):
                        last = te.transpose(PBQ[:, hp * 128:(hp + 1) * 128],
                                            sQt[:, qt_ + hp * 128:qt_ + hp * 128 + 128],
                                            sCB[:, 2048:2176])
                    for hp in range(HP):
                        last = te.transpose(PBK[:, hp * 128:(hp + 1) * 128],
                                            sKp[:, kp_ + hp * 128:kp_ + hp * 128 + 128],
                                            sCB[:, 2048:2176])
                    pe.inc(last)
                    M[("ropet", t_)] = pe.n
                    guard[("sQt", t_ % 2)] = (pe, pe.n)
                    guard[("sKp", t_ % 2)] = (pe, pe.n)
                    sc_ = rec.scalar
                    qf_ = (t_ % 2) * C
                    _w(sc_, pe, M[("ropet", t_)])
                    need(sc_, ("sQf", t_ % 2))
                    sc_.copy(sQf[:, qf_:qf_ + 512], PBQ[:, 0:512])
                    ac.inc(sc_.copy(sQf[:, qf_ + 512:qf_ + 1024], PBQ[:, 512:1024]))
                    M[("qfev", t_)] = ac.n
                    ve3 = rec.vector
                    kf_ = (t_ % 2) * C
                    _w(ve3, pe, M[("ropet", t_)])
                    need(ve3, ("sKf", t_ % 2))
                    ve3.tensor_copy(sKf[:, kf_:kf_ + 512], PBK[:, 0:512])
                    dv.inc(ve3.tensor_copy(sKf[:, kf_ + 512:kf_ + 1024],
                                           PBK[:, 512:1024]))
                    M[("kfev", t_)] = dv.n

                if ok(tG2):
                    t_ = tG2
                    te2 = rec.tensor
                    _w(te2, dv, M[("gnh0", t_)])
                    if t_ == 0:
                        _w(te2, dv, M["wo_ready"])
                        _w(te2, pl, 3)
                    if it >= 1:
                        lq = min(it - 7, nt - 1)
                        if lq >= 0:
                            _w(te2, dv, M[("w1", lq, HP - 1)])
                        lx = min(it - 2, nt - 1)
                        if lx >= 0:
                            _w(te2, ac, M[("xnf", lx)])
                            _w(te2, dv, M[("xnb", lx)])
                    for hp in range(HP // 2):
                        te2.matmul(B[0][:], sGn[:, hp * 128:hp * 128 + 128],
                                   sWo[:, hp * C: hp * C + 512],
                                   start=(hp == 0), stop=False)
                        te2.matmul(B[1][:], sGn[:, hp * 128:hp * 128 + 128],
                                   sWo[:, hp * C + 512: hp * C + 1024],
                                   start=(hp == 0), stop=False)
                    _w(te2, dv, M[("gn", t_)])
                    for hp in range(HP // 2, HP):
                        te2.matmul(B[0][:], sGn[:, hp * 128:hp * 128 + 128],
                                   sWo[:, hp * C: hp * C + 512],
                                   start=False, stop=False)
                        te2.matmul(B[1][:], sGn[:, hp * 128:hp * 128 + 128],
                                   sWo[:, hp * C + 512: hp * C + 1024],
                                   start=False, stop=False)
                    te2.matmul(B[0][:], sOnes1[0:1, :], sBeta[0:1, 0:512],
                               start=False, stop=True)
                    pe.inc(te2.matmul(B[1][:], sOnes1[0:1, :], sBeta[0:1, 512:1024],
                                      start=False, stop=True))
                    M[("attn", t_)] = pe.n
                    guard[("sGn", 0)] = (pe, pe.n)
                    ve2 = rec.vector
                    x2_ = (t_ % 2) * C
                    ln_ = (t_ % 10) * 12
                    _w(ve2, pe, M[("attn", t_)])
                    _w(ve2, dx2, M[("dx2", t_)])
                    if t_ >= 2:
                        _w(ve2, dop, M[("dop", t_ - 2)])
                    ve2.scalar_tensor_tensor(
                        sOutp[:, x2_:x2_ + 512], sX2[:, x2_:x2_ + 512],
                        sLn[:, ln_ + 8:ln_ + 9], B[0][:], ALU.mult, ALU.add)
                    dv.inc(ve2.scalar_tensor_tensor(
                        sOutp[:, x2_ + 512:x2_ + 1024], sX2[:, x2_ + 512:x2_ + 1024],
                        sLn[:, ln_ + 8:ln_ + 9], B[1][:], ALU.mult, ALU.add))
                    _w(ve2, dv)
                    ve2.tensor_scalar_add(sOutp[:, x2_:x2_ + 512],
                                          sOutp[:, x2_:x2_ + 512],
                                          sLn[:, ln_ + 9:ln_ + 10])
                    dv.inc(ve2.tensor_scalar_add(sOutp[:, x2_ + 512:x2_ + 1024],
                                                 sOutp[:, x2_ + 512:x2_ + 1024],
                                                 sLn[:, ln_ + 9:ln_ + 10]))
                    M[("outp", t_)] = dv.n
                    guard[("sX2", t_ % 2)] = (dv, dv.n)
                    guard[("sLn", t_ % 10)] = (dv, dv.n)
                    sy2 = rec.sync
                    _w(sy2, dv, M[("outp", t_)])
                    _w(sy2, dop, dop.n)
                    dop.inc(sy2.dma_start(out=out[t_ * 128:(t_ + 1) * 128, :],
                                          in_=sOutp[:, x2_:x2_ + C]), 16)
                    M[("dop", t_)] = dop.n

                if ok(tG1):
                    t_ = tG1
                    a2_ = 0
                    st_ = 0
                    te = rec.tensor
                    if t_ == 0:
                        _w(te, pl, 4)
                    _w(te, ac, M[("av2", t_)])
                    _w(te, dv, M[("avb", t_)])
                    _w(te, dv, M[("w1", min(t_, nt - 1), HP - 1)])
                    if ok(tG2):
                        _w(te, dv, M[("gn", tG2)])
                    te.matmul(B[2][0:2, :], sBlk2[:], sAvB[:, a2_:a2_ + 512],
                              start=True, stop=True)
                    pe.inc(te.matmul(B[3][0:2, :], sBlk2[:], sAv2[:, a2_:a2_ + 512],
                                     start=True, stop=True))
                    M[("st1", t_)] = pe.n
                    sc2 = rec.scalar
                    _w(sc2, pe, M[("st1", t_)])
                    need(sc2, ("sSt", 0))
                    sc2.copy(sSt[0:2, st_:st_ + 512], B[2][0:2, :])
                    ac.inc(sc2.copy(sSt[0:2, st_ + 1024:st_ + 1536], B[3][0:2, :]))
                    M[("ste1", t_)] = ac.n
                    te2 = rec.tensor
                    _w(te2, ac, M[("ste1", t_)])
                    te2.matmul(B[2][0:2, :], sBlk2[:], sAvB[:, a2_ + 512:a2_ + 1024],
                               start=True, stop=True)
                    pe.inc(te2.matmul(B[3][0:2, :], sBlk2[:],
                                      sAv2[:, a2_ + 512:a2_ + 1024],
                                      start=True, stop=True))
                    M[("st2", t_)] = pe.n
                    guard[("sAvB", 0)] = (pe, pe.n)
                    guard[("sAv2", 0)] = (pe, pe.n)
                    sc3 = rec.scalar
                    _w(sc3, pe, M[("st2", t_)])
                    sc3.copy(sSt[0:2, st_ + 512:st_ + 1024], B[2][0:2, :])
                    ac.inc(sc3.copy(sSt[0:2, st_ + 1536:st_ + 2048], B[3][0:2, :]))
                    M[("ste2", t_)] = ac.n
                    ve8 = rec.vector
                    _w(ve8, ac, M[("ste2", t_)])
                    need(ve8, ("sStB", 0))
                    dv.inc(ve8.tensor_copy(sStB[0:2, st_:st_ + 1024],
                                           sSt[0:2, st_:st_ + 1024]))
                    _w(ve8, dv)
                    dv.inc(ve8.tensor_mul(sSt[0:2, st_:st_ + 1024],
                                          sSt[0:2, st_:st_ + 1024],
                                          sSt[0:2, st_:st_ + 1024]))
                    _w(ve8, dv)
                    dv.inc(ve8.tensor_sub(sSt[0:2, st_ + 1024:st_ + 2048],
                                          sSt[0:2, st_ + 1024:st_ + 2048],
                                          sSt[0:2, st_:st_ + 1024]))
                    M[("var", t_)] = dv.n

                if ok(tGP):
                    t_ = tGP
                    xnf_ = (t_ % 2) * C
                    te = rec.tensor
                    if t_ == 0:
                        _w(te, dw, 16)
                    _w(te, ac, M[("xnf", t_)])
                    _w(te, dv, M[("xnb", t_)])
                    if ok(tG2):
                        _w(te, dv, M[("gn", tG2)])
                    lre = min(it - 7, nt - 1)
                    if lre >= 0:
                        _w(te, ac, M[("avev", lre)])
                    if t_ >= 1:
                        _w(te, dv, M[("vev", t_ - 1)])
                        _w(te, ac, M[("qev", t_ - 1)])
                    last = None
                    for j in range(CH):
                        last = te.matmul(B[4][0:8, 0:128],
                                         sCF[:, 162 + j * E:162 + (j + 1) * E],
                                         sXnF[:, xnf_ + j * 128:xnf_ + j * 128 + 128],
                                         start=(j == 0), stop=(j == CH - 1))
                    pe.inc(last)
                    M[("gate", t_)] = pe.n
                    guard[("sXnF", t_ % 2)] = (pe, pe.n)
                    sc_ = rec.scalar
                    _w(sc_, pe, M[("gate", t_)])
                    need(sc_, ("sSc", 0))
                    ac.inc(sc_.copy(sSc[:, 0:128], B[4][0:8, 0:128]))
                    M[("scev", t_)] = ac.n
                    te1b = rec.tensor
                    _w(te1b, ac, M[("scev", t_)])
                    pe.inc(te1b.transpose(B[4][:, 128:136], sSc[:, 0:128],
                                          sCF[0:8, 0:8]))
                    M[("scT", t_)] = pe.n
                    guard[("sSc", 0)] = (pe, pe.n)
                    ve = rec.vector
                    g_ = 0
                    if t_ == 0:
                        _w(ve, dw, 16)
                    _w(ve, pe, M[("scT", t_)])
                    need(ve, ("sGsc", 0))
                    dv.inc(ve.tensor_copy(sGsc[:, g_:g_ + 8], B[4][:, 128:136]))
                    M[("maskcp", t_)] = dv.n
                    _w(ve, dv)
                    dv.inc(ve.reduce_max(sGsc[:, g_ + 8:g_ + 9], sGsc[:, g_:g_ + 8],
                                         axis=AX.X))
                    _w(ve, dv)
                    dv.inc(ve.tensor_single_scalar(sGsc[:, g_ + 16:g_ + 24],
                                                   sGsc[:, g_:g_ + 8],
                                                   sGsc[:, g_ + 8:g_ + 9],
                                                   ALU.is_equal))
                    _w(ve, dv)
                    dv.inc(ve.scalar_tensor_tensor(sGsc[:, g_ + 24:g_ + 32],
                                                   sGsc[:, g_ + 16:g_ + 24],
                                                   -1000.0, sCF[:, 136:144],
                                                   ALU.mult, ALU.add))
                    _w(ve, dv)
                    dv.inc(ve.tensor_reduce(sGsc[:, g_ + 9:g_ + 10],
                                            sGsc[:, g_ + 24:g_ + 32],
                                            axis=AX.X, op=ALU.min))
                    _w(ve, dv)
                    dv.inc(ve.tensor_single_scalar(
                        sMask[:, t_ * E:(t_ + 1) * E], sGsc[:, g_ + 24:g_ + 32],
                        sGsc[:, g_ + 9:g_ + 10], ALU.is_equal))
                    M[("mask", t_)] = dv.n
                    guard[("sGsc", 0)] = (dv, dv.n)

                if ok(tLN):
                    t_ = tLN
                    x_ = (t_ % 2) * C
                    ln_ = (t_ % 10) * 12
                    sc_ = rec.scalar
                    _w(sc_, dx, M[("dx", t_)])
                    need(sc_, ("sLn", t_ % 10))
                    _w(sc_, ac, ac.n)
                    ac.inc(sc_.activation(sDump[:], sX[:, x_:x_ + C], AF.Square,
                                          accum_out=sLn[:, ln_ + 1:ln_ + 2]))
                    M[("lnsq", t_)] = ac.n
                    ve = rec.vector
                    _w(ve, dx, M[("dx", t_)])
                    _w(ve, ac, M[("lnsq", t_)])
                    need(ve, ("sLn", t_ % 10))
                    dv.inc(ve.reduce_sum(sLn[:, ln_:ln_ + 1], sX[:, x_:x_ + C],
                                         axis=AX.X))
                    _w(ve, dv)
                    ve.tensor_scalar_mul(sLn[:, ln_ + 2:ln_ + 3],
                                         sLn[:, ln_:ln_ + 1], 1.0 / C)
                    dv.inc(ve.tensor_scalar_mul(sLn[:, ln_ + 3:ln_ + 4],
                                                sLn[:, ln_ + 1:ln_ + 2], 1.0 / C))
                    _w(ve, dv)
                    dv.inc(ve.tensor_mul(sLn[:, ln_ + 4:ln_ + 5],
                                         sLn[:, ln_ + 2:ln_ + 3],
                                         sLn[:, ln_ + 2:ln_ + 3]))
                    _w(ve, dv)
                    dv.inc(ve.tensor_sub(sLn[:, ln_ + 5:ln_ + 6],
                                         sLn[:, ln_ + 3:ln_ + 4],
                                         sLn[:, ln_ + 4:ln_ + 5]))
                    M[("lnvar", t_)] = dv.n

                if ok(tLN2):
                    t_ = tLN2
                    x_ = (t_ % 2) * C
                    ln_ = (t_ % 10) * 12
                    sc2 = rec.scalar
                    if t_ == 0:
                        _w(sc2, dw, 16)
                    _w(sc2, dv, M[("lnvar", t_)])
                    ac.inc(sc2.activation(sLn[:, ln_ + 6:ln_ + 7],
                                          sLn[:, ln_ + 5:ln_ + 6],
                                          AF.Ln, bias=sCF[:, 144:145]))
                    _w(sc2, ac)
                    ac.inc(sc2.activation(sLn[:, ln_ + 7:ln_ + 8],
                                          sLn[:, ln_ + 6:ln_ + 7], AF.Exp,
                                          scale=sCF[:, 145:146]))
                    M[("lnr", t_)] = ac.n
                    ve2 = rec.vector
                    _w(ve2, ac, M[("lnr", t_)])
                    ve2.tensor_scalar_add(sLn[:, ln_ + 8:ln_ + 9],
                                          sLn[:, ln_ + 7:ln_ + 8], 1.0)
                    dv.inc(ve2.scalar_tensor_tensor(
                        sLn[:, ln_ + 9:ln_ + 10], sLn[:, ln_ + 2:ln_ + 3], -1.0,
                        sLn[:, ln_ + 7:ln_ + 8], ALU.mult, ALU.mult))
                    need(ve2, ("sXn", t_ % 2))
                    _w(ve2, dv)
                    xn_ = (t_ % 2) * C
                    dv.inc(ve2.tensor_scalar(sXn[:, xn_:xn_ + C], sX[:, x_:x_ + C],
                                             sLn[:, ln_ + 2:ln_ + 3],
                                             sLn[:, ln_ + 7:ln_ + 8],
                                             ALU.subtract, ALU.mult))
                    M[("xn", t_)] = dv.n
                    guard[("sX", t_ % 2)] = (dv, dv.n)

                if ok(tG1):
                    t_ = tG1
                    st_ = 0
                    sc4 = rec.scalar
                    _w(sc4, dv, M[("var", t_)])
                    ac.inc(sc4.activation(sSt[0:2, st_ + 1024:st_ + 2048],
                                          sSt[0:2, st_ + 1024:st_ + 2048], AF.Ln,
                                          bias=sCF[0:2, 144:145]))
                    _w(sc4, ac)
                    ac.inc(sc4.activation(sSt[0:2, st_ + 1024:st_ + 2048],
                                          sSt[0:2, st_ + 1024:st_ + 2048], AF.Exp,
                                          scale=sCF[0:2, 145:146]))
                    M[("rstd", t_)] = ac.n
                    ve9 = rec.vector
                    _w(ve9, ac, M[("rstd", t_)])
                    dv.inc(ve9.tensor_copy(sStB[0:2, st_ + 1024:st_ + 2048],
                                           sSt[0:2, st_ + 1024:st_ + 2048]))
                    M[("stb", t_)] = dv.n
                    guard[("sSt", 0)] = (dv, dv.n)

                if ok(tGP):
                    t_ = tGP
                    xnf_ = (t_ % 2) * C
                    xnb_ = (t_ % 3) * C
                    teq1 = rec.tensor
                    if t_ == 0:
                        _w(teq1, dwg, 16)
                    _w(teq1, dv, M[("xnb", t_)])
                    _w(teq1, ac, M[("xnf", t_)])
                    if ok(tG2):
                        _w(teq1, dv, M[("gn", tG2)])
                    lastq = None
                    for j in range(CH):
                        lhs = sXnB[:, xnb_ + j * 128:xnb_ + j * 128 + 128]
                        lastq = teq1.matmul(B[5][:], lhs,
                                            sWq[:, j * C + 512:j * C + 1024],
                                            start=(j == 0), stop=(j == CH - 1))
                    pe.inc(lastq)
                    M[("projq1", t_)] = pe.n
                    scq1 = rec.scalar
                    q_ = (t_ % 2) * C
                    _w(scq1, pe, M[("projq1", t_)])
                    need(scq1, ("sQ", t_ % 2))
                    ac.inc(scq1.copy(sQ[:, q_ + 512:q_ + 1024], B[5][:]))
                    M[("qev1", t_)] = ac.n
                    te3 = rec.tensor
                    if t_ == 0:
                        _w(te3, dwk, 16)
                    if t_ == 3:
                        _w(te3, dv, M["beta_evac"])
                    if ok(tG1):
                        _w(te3, ac, M[("ste2", tG1)])
                    elif t_ >= 1:
                        _w(te3, ac, M[("qev", t_ - 1)])
                    lastk = None
                    for j in range(CH):
                        lhs = sXnB[:, xnb_ + j * 128:xnb_ + j * 128 + 128]
                        lastk = te3.matmul(B[2][:], lhs, sWk[:, j * C:j * C + 512],
                                           start=(j == 0), stop=(j == CH - 1))
                        lastk = te3.matmul(B[3][:], lhs,
                                           sWk[:, j * C + 512:j * C + 1024],
                                           start=(j == 0), stop=(j == CH - 1))
                    pe.inc(lastk)
                    M[("projk", t_)] = pe.n
                    sc3 = rec.scalar
                    k_ = (t_ % 2) * C
                    _w(sc3, pe, M[("projk", t_)])
                    need(sc3, ("sK", t_ % 2))
                    sc3.copy(sK[:, k_:k_ + 512], B[2][:])
                    ac.inc(sc3.copy(sK[:, k_ + 512:k_ + 1024], B[3][:]))
                    M[("kev", t_)] = ac.n
                    te4 = rec.tensor
                    if t_ == 0:
                        _w(te4, dwv, 16)
                    _w(te4, dv, M[("maskcp", t_)])
                    _w(te4, ac, M[("qev1", t_)])
                    lastv = None
                    for j in range(CH):
                        lhs = sXnB[:, xnb_ + j * 128:xnb_ + j * 128 + 128]
                        lastv = te4.matmul(B[4][:], lhs, sWv[:, j * C:j * C + 512],
                                           start=(j == 0), stop=(j == CH - 1))
                        lastv = te4.matmul(B[5][:], lhs,
                                           sWv[:, j * C + 512:j * C + 1024],
                                           start=(j == 0), stop=(j == CH - 1))
                    pe.inc(lastv)
                    M[("projv", t_)] = pe.n
                    if not ok(tRE):
                        ve0 = rec.vector
                        v_ = (t_ % 4) * C
                        _w(ve0, pe, M[("projv", t_)])
                        need(ve0, ("sV", t_ % 4))
                        ve0.tensor_copy(sV[:, v_:v_ + 512], B[4][:])
                        dv.inc(ve0.tensor_copy(sV[:, v_ + 512:v_ + 1024], B[5][:]))
                        M[("vev", t_)] = dv.n
                    teq0 = rec.tensor
                    _w(teq0, ac, M[("kev", t_)])
                    lastq = None
                    for j in range(CH):
                        lhs = sXnB[:, xnb_ + j * 128:xnb_ + j * 128 + 128]
                        lastq = teq0.matmul(B[2][:], lhs, sWq[:, j * C:j * C + 512],
                                            start=(j == 0), stop=(j == CH - 1))
                    pe.inc(lastq)
                    M[("projq", t_)] = pe.n
                    guard[("sXnB", t_ % 3)] = (pe, pe.n)
                    scq0 = rec.scalar
                    _w(scq0, pe, M[("projq", t_)])
                    ac.inc(scq0.copy(sQ[:, q_:q_ + 512], B[2][:]))
                    M[("qev", t_)] = ac.n

                if ok(tRP):
                    t_ = tRP
                    q_ = (t_ % 2) * C
                    k_ = (t_ % 2) * C
                    qt_ = (t_ % 2) * C
                    kt2_ = (t_ % 3) * C
                    co = t_ * DH
                    cosb = bass.AP(sCB, 2176 + co, [[WB, 128], [0, H], [1, DH]])
                    ssb0 = bass.AP(sCB, 4224 + co, [[WB, 128], [0, H], [1, 32]])
                    ssb1 = bass.AP(sCB, 4256 + co, [[WB, 128], [0, H], [1, 32]])
                    rr0 = bass.AP(sRc, 0, [[C, 128], [DH, H], [1, 32]])
                    rr1 = bass.AP(sRc, 32, [[C, 128], [DH, H], [1, 32]])
                    po2 = rec.gpsimd
                    if t_ == 0:
                        _w(po2, dw, 32)
                    _w(po2, ac, M[("kev", t_)])
                    kp_ = (t_ % 2) * C
                    need(po2, ("sKp", t_ % 2))
                    kc3 = bass.AP(sKp, kp_, [[2 * C, 128], [DH, H], [1, DH]])
                    k3 = bass.AP(sK, k_, [[2 * C, 128], [DH, H], [1, DH]])
                    po2.tensor_tensor(kc3, k3, cosb, ALU.mult)
                    kh0 = bass.AP(sK, k_, [[2 * C, 128], [DH, H], [1, 32]])
                    kh1 = bass.AP(sK, k_ + 32, [[2 * C, 128], [DH, H], [1, 32]])
                    po2.tensor_tensor(rr0, kh1, ssb0, ALU.mult)
                    pl.inc(po2.tensor_tensor(rr1, kh0, ssb1, ALU.mult))
                    _w(po2, pl)
                    pl.inc(po2.tensor_add(sKp[:, kp_:kp_ + C],
                                          sKp[:, kp_:kp_ + C], sRc[:]))
                    _w(po2, pl)
                    need(po2, ("sKtB", t_ % 3))
                    pl.inc(po2.tensor_mul(sKtB[:, kt2_:kt2_ + C],
                                          sKp[:, kp_:kp_ + C],
                                          sCB[:, 1024:2048]))
                    M[("ropek", t_)] = pl.n
                    guard[("sK", t_ % 2)] = (pl, pl.n)
                    guard[("sKp", t_ % 2)] = (pl, pl.n)
                    po = rec.gpsimd
                    _w(po, ac, M[("qev", t_)])
                    _w(po, ac, M[("qev1", t_)])
                    _w(po, pl, M[("ropek", t_)])
                    need(po, ("sQt", t_ % 2))
                    qc3 = bass.AP(sQt, qt_, [[2 * C, 128], [DH, H], [1, DH]])
                    q3 = bass.AP(sQ, q_, [[2 * C, 128], [DH, H], [1, DH]])
                    po.tensor_tensor(qc3, q3, cosb, ALU.mult)
                    qh0 = bass.AP(sQ, q_, [[2 * C, 128], [DH, H], [1, 32]])
                    qh1 = bass.AP(sQ, q_ + 32, [[2 * C, 128], [DH, H], [1, 32]])
                    po.tensor_tensor(rr0, qh1, ssb0, ALU.mult)
                    pl.inc(po.tensor_tensor(rr1, qh0, ssb1, ALU.mult))
                    _w(po, pl)
                    pl.inc(po.tensor_add(sQt[:, qt_:qt_ + C],
                                         sQt[:, qt_:qt_ + C], sRc[:]))
                    _w(po, pl)
                    pl.inc(po.tensor_mul(sQt[:, qt_:qt_ + C],
                                         sQt[:, qt_:qt_ + C],
                                         sCB[:, 0:1024]))
                    M[("ropeq", t_)] = pl.n
                    guard[("sQ", t_ % 2)] = (pl, pl.n)

                if ok(tRE):
                    t_ = tRE
                    qf_ = (t_ % 2) * C
                    kf_ = (t_ % 2) * C
                    v_ = (t_ % 4) * C
                    kt_ = (t_ % 3) * C
                    av_ = (t_ % 2) * C

                    def emit_qk(hp):
                        qb = [B[0], B[1]] if hp % 2 == 0 else [B[2], B[3]]
                        te = rec.tensor
                        if hp == 0:
                            _w(te, ac, M[("qfev", t_)])
                            _w(te, dv, M[("kfev", t_)])
                            lxt = min(tXT, nt - 1)
                            _w(te, ac, M[("xnf", lxt)])
                            _w(te, dv, M[("xnb", lxt)])
                            if ok(tG2):
                                _w(te, dv, M[("outp", tG2)])
                        if hp == 1:
                            _w(te, ac, M[("qev", min(tGP, nt - 1))])
                            if ok(tG1):
                                _w(te, ac, M[("ste2", tG1)])
                            if ok(tG2):
                                _w(te, dv, M[("gn", tG2)])
                        if hp >= 2:
                            _w(te, dv, M[("w1", t_, hp - 2)])
                        te.matmul(qb[0][:, 0:128],
                                  sKf[0:64, kf_ + hp * 128:kf_ + hp * 128 + 128],
                                  sQf[0:64, qf_ + hp * 128:qf_ + hp * 128 + 128],
                                  start=True, stop=True, tile_position=(0, 0))
                        pe.inc(te.matmul(
                            qb[1][:, 0:128],
                            sKf[64:128, kf_ + hp * 128:kf_ + hp * 128 + 128],
                            sQf[64:128, qf_ + hp * 128:qf_ + hp * 128 + 128],
                            start=True, stop=True, tile_position=(64, 0)))
                        M[("qk", t_, hp)] = pe.n

                    def emit_w1(hp):
                        hA, hB = 2 * hp, 2 * hp + 1
                        w_ = (hp % 2) * 256
                        qb = [B[0], B[1]] if hp % 2 == 0 else [B[2], B[3]]
                        ve = rec.vector
                        if t_ == 0 and hp == 0:
                            _w(ve, dw, 32)
                        _w(ve, pe, M[("qk", t_, hp)])
                        ve.tensor_tensor(sW1[:, w_:w_ + 128], qb[0][:, 0:128],
                                         sCB[:, 6272 + hA * 128:6400 + hA * 128],
                                         ALU.mult)
                        dv.inc(ve.tensor_tensor(sW1[:, w_ + 128:w_ + 256],
                                                qb[1][:, 0:128],
                                                sCB[:, 6272 + hB * 128:
                                                    6400 + hB * 128],
                                                ALU.mult))
                        M[("w1", t_, hp)] = dv.n

                    def emit_av(hp):
                        hA, hB = 2 * hp, 2 * hp + 1
                        w_ = (hp % 2) * 256
                        te2 = rec.tensor
                        _w(te2, dv, M[("w1", t_, hp)])
                        if hp == 0:
                            _w(te2, dv, M[("vev", min(tGP, nt - 1))])
                        if hp == 0 and t_ >= 1:
                            _w(te2, ac, M[("avev", t_ - 1)])
                        if hp > 0:
                            _w(te2, ac, M[("avev1", t_, hp - 1)])
                        te2.matmul(B[4][0:64, 0:128],
                                   sV[:, v_ + hA * 64:v_ + hA * 64 + 64],
                                   sW1[:, w_:w_ + 128], start=True,
                                   stop=(t_ == 0), tile_position=(0, 0),
                                   skip_group_check=True)
                        te2.matmul(B[4][64:128, 0:128],
                                   sV[:, v_ + hB * 64:v_ + hB * 64 + 64],
                                   sW1[:, w_ + 128:w_ + 256], start=True,
                                   stop=(t_ == 0), tile_position=(0, 64),
                                   skip_group_check=True)
                        if t_ > 0:
                            _w(te2, ac, M[("sb", t_ - 1, hp)])
                            te2.matmul(B[4][0:64, 0:128],
                                       sSb[0:64, hp * 64:(hp + 1) * 64],
                                       sQf[0:64, qf_ + hp * 128:qf_ + hp * 128 + 128],
                                       start=False, stop=True, tile_position=(0, 0),
                                       skip_group_check=True)
                            te2.matmul(B[4][64:128, 0:128],
                                       sSb[64:128, hp * 64:(hp + 1) * 64],
                                       sQf[64:128, qf_ + hp * 128:qf_ + hp * 128 + 128],
                                       start=False, stop=True,
                                       tile_position=(64, 64),
                                       skip_group_check=True)
                        if hp > 1:
                            _w(te2, dv, M[("supd", t_, hp - 2)])
                        kvo = (hp % 2) * 64
                        te2.matmul(B[5][0:64, kvo:kvo + 64],
                                   sKtB[:, kt_ + hA * 64:kt_ + hA * 64 + 64],
                                   sV[:, v_ + hA * 64:v_ + hA * 64 + 64],
                                   start=True, stop=True, tile_position=(0, 0),
                                   skip_group_check=True)
                        pe.inc(te2.matmul(
                            B[5][64:128, kvo:kvo + 64],
                            sKtB[:, kt_ + hB * 64:kt_ + hB * 64 + 64],
                            sV[:, v_ + hB * 64:v_ + hB * 64 + 64],
                            start=True, stop=True, tile_position=(0, 64),
                            skip_group_check=True))
                        M[("avkv", t_, hp)] = pe.n
                        sc_ = rec.scalar
                        _w(sc_, pe, M[("avkv", t_, hp)])
                        if hp == 0:
                            need(sc_, ("sAv", t_ % 2))
                        ac.inc(sc_.copy(sAv[:, av_ + hp * 128:av_ + hp * 128 + 128],
                                        B[4][:, 0:128]))
                        M[("avev1", t_, hp)] = ac.n
                        ve2 = rec.vector
                        _w(ve2, pe, M[("avkv", t_, hp)])
                        if t_ == 0:
                            _w(ve2, pl, 2)
                        if t_ >= 1:
                            _w(ve2, ac, M[("sb", t_ - 1, hp)])
                        dv.inc(ve2.scalar_tensor_tensor(
                            sS[:, hp * 64:(hp + 1) * 64],
                            sS[:, hp * 64:(hp + 1) * 64],
                            sCF[:, 128 + hp:129 + hp], B[5][:, kvo:kvo + 64],
                            ALU.mult, ALU.add))
                        M[("supd", t_, hp)] = dv.n
                        sc2_ = rec.scalar
                        _w(sc2_, dv, M[("supd", t_, hp)])
                        ac.inc(sc2_.copy(sSb[:, hp * 64:(hp + 1) * 64],
                                         sS[:, hp * 64:(hp + 1) * 64]))
                        M[("sb", t_, hp)] = ac.n

                    emit_qk(0)
                    emit_w1(0)
                    emit_qk(1)
                    emit_w1(1)
                    if ok(tGP):
                        tv = tGP
                        ve0 = rec.vector
                        vv_ = (tv % 4) * C
                        _w(ve0, pe, M[("projv", tv)])
                        need(ve0, ("sV", tv % 4))
                        ve0.tensor_copy(sV[:, vv_:vv_ + 512], B[4][:])
                        dv.inc(ve0.tensor_copy(sV[:, vv_ + 512:vv_ + 1024],
                                               B[5][:]))
                        M[("vev", tv)] = dv.n
                    for hp in range(HP):
                        emit_av(hp)
                        if hp + 2 < HP:
                            emit_qk(hp + 2)
                            emit_w1(hp + 2)
                    M[("avev", t_)] = M[("avev1", t_, HP - 1)]
                    M[("ret", t_)] = M[("avkv", t_, HP - 1)]
                    guard[("sQf", t_ % 2)] = (pe, M[("ret", t_)])
                    guard[("sKf", t_ % 2)] = (pe, M[("qk", t_, HP - 1)])
                    guard[("sV", t_ % 4)] = (pe, M[("ret", t_)])
                    guard[("sKtB", t_ % 3)] = (pe, M[("ret", t_)])

                if ok(tRE):
                    t_ = tRE
                    av_ = (t_ % 2) * C
                    sc_ = rec.scalar
                    _w(sc_, ac, M[("avev", t_)])
                    need(sc_, ("sAv2", 0))
                    ac.inc(sc_.activation(sAv2[:, 0:C], sAv[:, av_:av_ + C],
                                          AF.Square))
                    M[("av2", t_)] = ac.n
                    veA = rec.vector
                    need(veA, ("sAvB", 0))
                    _w(veA, ac, M[("avev", t_)])
                    dv.inc(veA.tensor_copy(sAvB[:, 0:C], sAv[:, av_:av_ + C]))
                    M[("avb", t_)] = dv.n

                if it == 0:
                    scg = rec.scalar
                    _w(scg, dw, 16)
                    ac.inc(scg.copy(sGnbB[:], sCF[:, 154:162]))
                    M["gnbB"] = ac.n

                if it == 5:
                    teb = rec.tensor
                    _w(teb, ac, M[("qev", 2)])
                    _w(teb, ac, M[("kev", 2)])
                    _w(teb, dwo, 16)
                    _w(teb, ac, M["gnbB"])
                    for j in range(CH):
                        teb.matmul(B[2][0:1, :], sGnbB[:, j:j + 1],
                                   sWo[:, j * C: j * C + 512],
                                   start=(j == 0), stop=(j == CH - 1))
                        last = teb.matmul(B[3][0:1, :], sGnbB[:, j:j + 1],
                                          sWo[:, j * C + 512: j * C + 1024],
                                          start=(j == 0), stop=(j == CH - 1))
                    pe.inc(last)
                    M["beta_pe"] = pe.n
                    veb = rec.vector
                    _w(veb, pe, M["beta_pe"])
                    _w(veb, dw, 16)
                    veb.tensor_copy(sBeta[0:1, 0:512], B[2][0:1, :])
                    dv.inc(veb.tensor_copy(sBeta[0:1, 512:1024], B[3][0:1, :]))
                    M["beta_evac"] = dv.n
                    last = None
                    for j in range(CH):
                        last = veb.tensor_scalar_mul(
                            sWo[:, j * C:(j + 1) * C], sWo[:, j * C:(j + 1) * C],
                            sCF[:, 146 + j:147 + j])
                    dv.inc(last)
                    M["wo_ready"] = dv.n

                if it == nt + 3 and _on("ALL"):
                    pog = rec.gpsimd
                    _w(pog, ac, M[("qev", nt - 1)])
                    _w(pog, dwe, dwe.n)
                    dwe.inc(pog.dma_start(
                        out=sWq[:].rearrange("p (j o) -> p j o", j=CH),
                        in_=bass.AP(ewT, 0,
                                    [[C, 128], [128 * C, CH], [1, C]])), 16)
                    pre_we[0] = dwe.n
                    _w(pog, ac, M[("kev", nt - 1)])
                    _w(pog, dwe, dwe.n)
                    dwe.inc(pog.dma_start(
                        out=sWk[:].rearrange("p (j o) -> p j o", j=CH),
                        in_=bass.AP(ewT, C * C,
                                    [[C, 128], [128 * C, CH], [1, C]])), 16)
                    pre_we[1] = dwe.n
                    scp = rec.scalar
                    _w(scp, dv, M[("vev", nt - 1)])
                    _w(scp, dxn, M[("dxn", 3)])
                    _w(scp, dxe, dxe.n)
                    dxe.inc(scp.dma_start(
                        out=sWv[:, 0:4 * C].rearrange(
                            "p (tl cc) -> p tl cc", tl=4),
                        in_=xnd[0:512, :].rearrange(
                            "(tl p) cc -> p tl cc", p=128)), 16)
                    pre_xn[0] = dxe.n
                    _w(scp, dxn, M[("dxn", 7)])
                    _w(scp, dxe, dxe.n)
                    dxe.inc(scp.dma_start(
                        out=sWv[:, 4 * C:8 * C].rearrange(
                            "p (tl cc) -> p tl cc", tl=4),
                        in_=xnd[512:1024, :].rearrange(
                            "(tl p) cc -> p tl cc", p=128)), 16)
                    pre_xn[1] = dxe.n

                rec.flush()

            a_pe, a_dv, a_ac, a_pl = pe.n, dv.n, ac.n, pl.n
            a_dop, a_dxn = dop.n, dxn.n

        if not _on("ALL"):
            def _fin(sync):
                _w(sync, pe, pe.n)
                _w(sync, dv, dv.n)
                _w(sync, ac, ac.n)
                _w(sync, pl, pl.n)
                _w(sync, dop, dop.n)
                _w(sync, dxn, dxn.n)
                dop.inc(sync.dma_start(out=out[0:128, :], in_=sOutp[:, 0:C]), 16)
                _w(sync, dop, dop.n)
            block.sync(_fin)
            return nc
        stB = ExitStack()
        with stB:
            sAcc = stB.enter_context(nc.sbuf_tensor("sAcc", [128, nt * C], F32))

            NG = (nt + 3) // 4

            acc_ld = {}

            def _b_acc(scalar):
                _w(scalar, pe, a_pe)
                _w(scalar, dv, a_dv)
                _w(scalar, pl, a_pl)
                for i in range(nt):
                    _w(scalar, dop, M[("dop", i)])
                    _w(scalar, dx, dx.n)
                    dx.inc(scalar.dma_start(out=sAcc[:, i * C:(i + 1) * C],
                                            in_=out[i * 128:(i + 1) * 128, :]),
                           16)
                    acc_ld[i] = dx.n
            block.scalar(_b_acc)

            mm_done = {}
            stt_done = {}
            xnld = {}

            for e in range(E):
                wslot = e % 2
                sWE = sWq if wslot == 0 else sWk

                if e < 2:
                    we_cnt = pre_we[e]
                else:
                    def _b_we(gpsimd, e=e, wslot=wslot, sWE=sWE):
                        _w(gpsimd, pe, a_pe)
                        _w(gpsimd, dv, a_dv)
                        _w(gpsimd, ac, a_ac)
                        _w(gpsimd, pl, a_pl)
                        _w(gpsimd, pe, mm_done[(e - 2, nt - 1)])
                        _w(gpsimd, dwe, dwe.n)
                        dwe.inc(gpsimd.dma_start(
                            out=sWE[:].rearrange("p (j o) -> p j o", j=CH),
                            in_=bass.AP(ewT, e * C * C,
                                        [[C, 128], [128 * C, CH], [1, C]])), 16)
                    block.gpsimd(_b_we)
                    we_cnt = dwe.n

                for g in range(NG):
                    s_flat = e * NG + g
                    gslot = s_flat % 2

                    if s_flat < 2:
                        xnld[(e, g)] = pre_xn[s_flat]
                    else:
                        def _b_xn(sync, e=e, g=g, gslot=gslot, s_flat=s_flat):
                            _w(sync, pe, a_pe)
                            _w(sync, dv, a_dv)
                            _w(sync, ac, a_ac)
                            _w(sync, pl, a_pl)
                            _w(sync, dxn, a_dxn)
                            pe_, pg_ = divmod(s_flat - 2, NG)
                            last_i = min(pg_ * 4 + 3, nt - 1)
                            _w(sync, pe, mm_done[(pe_, last_i)])
                            ntl = min(4, nt - g * 4)
                            _w(sync, dxe, dxe.n)
                            dxe.inc(sync.dma_start(
                                out=sWv[:, gslot * 4 * C: gslot * 4 * C + ntl * C
                                        ].rearrange("p (tl cc) -> p tl cc", tl=ntl),
                                in_=xnd[g * 512:g * 512 + ntl * 128, :].rearrange(
                                    "(tl p) cc -> p tl cc", p=128)), 16)
                        block.sync(_b_xn)
                        xnld[(e, g)] = dxe.n

                    for i in range(g * 4, min(g * 4 + 4, nt)):
                        flat = e * nt + i
                        bb = (flat % 3) * 2
                        xoff = (gslot * 4 + (i - g * 4)) * C

                        def _b_mm(tensor, e=e, i=i, bb=bb, xoff=xoff, flat=flat,
                                  we_cnt=we_cnt, g=g, sWE=sWE):
                            _w(tensor, dwe, we_cnt)
                            _w(tensor, dxe, xnld[(e, g)])
                            if flat < 3:
                                _w(tensor, dv, a_dv)
                                _w(tensor, ac, a_ac)
                                _w(tensor, pl, a_pl)
                            if flat >= 3:
                                key = divmod(flat - 3, nt)
                                _w(tensor, dv, stt_done[key])
                            last = None
                            for j in range(CH):
                                for half in range(2):
                                    last = tensor.matmul(
                                        B[bb + half][:],
                                        sWv[:, xoff + j * 128: xoff + j * 128 + 128],
                                        sWE[:, j * C + half * 512:
                                            j * C + half * 512 + 512],
                                        start=(j == 0), stop=(j == CH - 1))
                            pe.inc(last)
                        block.tensor(_b_mm)
                        mm_done[(e, i)] = pe.n

                        def _b_stt(vector, e=e, i=i, bb=bb):
                            _w(vector, pe, mm_done[(e, i)])
                            if e == 0:
                                _w(vector, dx, acc_ld[i])
                            else:
                                _w(vector, dv, stt_done[(e - 1, i)])
                            vector.scalar_tensor_tensor(
                                sAcc[:, i * C:i * C + 512],
                                B[bb][:], sMask[:, i * E + e:i * E + e + 1],
                                sAcc[:, i * C:i * C + 512], ALU.mult, ALU.add)
                            dv.inc(vector.scalar_tensor_tensor(
                                sAcc[:, i * C + 512:(i + 1) * C],
                                B[bb + 1][:], sMask[:, i * E + e:i * E + e + 1],
                                sAcc[:, i * C + 512:(i + 1) * C],
                                ALU.mult, ALU.add))
                        block.vector(_b_stt)
                        stt_done[(e, i)] = dv.n

            def _b_out(sync):
                for i in range(nt):
                    _w(sync, dv, stt_done[(E - 1, i)])
                    dop.inc(sync.dma_start(out=out[i * 128:(i + 1) * 128, :],
                                           in_=sAcc[:, i * C:(i + 1) * C]), 16)
                _w(sync, dop)
                _w(sync, dxn)
            block.sync(_b_out)

    return nc


def _prep_inputs(L0_attn_x, L0_attn_Wq, L0_attn_Wk, L0_attn_Wv, L0_attn_Wo,
                 L0_attn_gn_weight, L0_attn_gn_bias, L0_ffn_gate_w,
                 L0_ffn_expert_w):
    cst = _host_consts()
    f32 = np.float32
    gnw = np.asarray(L0_attn_gn_weight, f32).reshape(CH, 128).T
    gnb = np.asarray(L0_attn_gn_bias, f32).reshape(CH, 128).T
    gw = np.asarray(L0_ffn_gate_w, f32)
    gwp = gw.T.reshape(CH, 128, E).transpose(1, 0, 2).reshape(128, CH * E)
    cf32 = np.concatenate([cst["identf"], cst["gp"], cst["cvec"], cst["epsv"],
                           gnw, gnb, gwp], axis=1).astype(f32)
    assert cf32.shape == (128, WF), cf32.shape
    common = dict(
        wqT=np.ascontiguousarray(np.asarray(L0_attn_Wq, f32).T),
        wkT=np.ascontiguousarray(np.asarray(L0_attn_Wk, f32).T),
        wvT=np.ascontiguousarray(np.asarray(L0_attn_Wv, f32).T),
        woT=np.ascontiguousarray(np.asarray(L0_attn_Wo, f32).T),
        ewT=np.ascontiguousarray(np.asarray(L0_ffn_expert_w, f32).transpose(0, 2, 1)),
        cb16=cst["cb16"],
        cf32=np.ascontiguousarray(cf32),
        sel2=cst["sel2"])
    xs = np.asarray(L0_attn_x, f32)
    in_maps = [dict(common, x=np.ascontiguousarray(xs[b])) for b in range(8)]
    return in_maps


def kernel(**inputs):
    in_maps = _prep_inputs(**inputs)
    nc = build_core()
    res = run_bass_kernel_spmd(nc, in_maps, core_ids=list(range(8)))
    return np.stack([r["out"] for r in res.results], axis=0)


# version 41
# speedup vs baseline: 1.0432x; 1.0432x over previous
import numpy as np
import ml_dtypes
from contextlib import ExitStack

import concourse.bass as bass
import concourse.mybir as mybir
from concourse.bass_utils import run_bass_kernel_spmd

F32 = mybir.dt.float32
BF16 = mybir.dt.bfloat16
AF = mybir.ActivationFunctionType
ALU = mybir.AluOpType
AX = mybir.AxisListType

T = 4096
C = 1024
H = 16
DH = 64
HP = 8
NT = 32
CH = 8
E = 8
GAMMAS = [1.0 - 2.0 ** (-5.0 - h) for h in range(H)]
WB = 8320
WF = 226


def _host_consts():
    f32 = np.float32
    bf = ml_dtypes.bfloat16
    inv_freq = (10000.0 ** (-(np.arange(0, DH, 2, dtype=np.float64) / DH)))
    tglob = np.arange(T, dtype=np.float64)
    ang = tglob[:, None] * inv_freq[None, :]
    emb = np.concatenate([ang, ang], axis=1)
    cos = np.cos(emb)
    sin = np.sin(emb)
    ssin = sin.copy()
    ssin[:, :DH // 2] *= -1.0
    cosT = np.zeros((128, NT * DH), dtype=bf)
    ssinT = np.zeros((128, NT * DH), dtype=bf)
    for i in range(NT):
        cosT[:, i * DH:(i + 1) * DH] = cos[i * 128:(i + 1) * 128, :].astype(bf)
        ssinT[:, i * DH:(i + 1) * DH] = ssin[i * 128:(i + 1) * 128, :].astype(bf)
    p = np.arange(128, dtype=np.float64)
    tqf = np.zeros((128, C), dtype=bf)
    tkf = np.zeros((128, C), dtype=bf)
    for h in range(H):
        g = GAMMAS[h]
        tqf[:, h * DH:(h + 1) * DH] = ((g ** p) / 8.0).astype(bf)[:, None]
        tkf[:, h * DH:(h + 1) * DH] = (g ** (128.0 - p)).astype(bf)[:, None]
    g1t = np.zeros((128, H * 128), dtype=bf)
    si = np.arange(128, dtype=np.float64)
    for h in range(H):
        g = GAMMAS[h]
        m = (np.arange(128)[None, :] >= si[:, None]).astype(np.float64)
        g1t[:, h * 128:(h + 1) * 128] = ((g ** (-si))[:, None] * m).astype(bf)
    gp = np.zeros((128, HP), dtype=f32)
    for hp in range(HP):
        gp[0:64, hp] = GAMMAS[2 * hp] ** 128.0
        gp[64:128, hp] = GAMMAS[2 * hp + 1] ** 128.0
    cvec = np.broadcast_to((np.arange(E) + 1000.0).astype(f32), (128, E)).copy()
    epsv = np.zeros((128, 2), dtype=f32)
    epsv[:, 0] = 1e-5
    epsv[:, 1] = -0.5
    identf = np.eye(128, dtype=f32)
    identb = np.eye(128, dtype=bf)
    sel2 = np.zeros((2, 128), dtype=bf)
    sel2[0, 0:64] = 1
    sel2[1, 64:128] = 1
    cb16 = np.concatenate([tqf, tkf, identb, cosT, ssinT, g1t], axis=1)
    return dict(cb16=np.ascontiguousarray(cb16), gp=gp, cvec=cvec, epsv=epsv,
                identf=identf, sel2=sel2)


class Tok:
    __slots__ = ("incs",)

    def __init__(self):
        self.incs = []

    def then_inc(self, h, k):
        self.incs.append((h, k))
        return self


class REng:
    def __init__(self, ops, name):
        object.__setattr__(self, "_ops", ops)
        object.__setattr__(self, "_wname", name)

    def wait_ge(self, h, v):
        self._ops.append(("w", h, v, None))

    def __getattr__(self, name):
        ops = self._ops

        def call(*a, **kw):
            tok = Tok()
            ops.append(("o", name, (a, kw), tok))
            return tok
        return call


class Rec:
    ENGINES = ("sync", "gpsimd", "scalar", "vector", "tensor")

    def __init__(self, block):
        self.block = block
        self.ops = {e: [] for e in self.ENGINES}

    def eng(self, name):
        return REng(self.ops[name], name)

    @property
    def sync(self):
        return self.eng("sync")

    @property
    def gpsimd(self):
        return self.eng("gpsimd")

    @property
    def scalar(self):
        return self.eng("scalar")

    @property
    def vector(self):
        return self.eng("vector")

    @property
    def tensor(self):
        return self.eng("tensor")

    def flush(self):
        for e in self.ENGINES:
            ops = self.ops[e]
            if not ops:
                continue

            def replay(eng, ops=list(ops)):
                for kind, x, y, tok in ops:
                    if kind == "w":
                        eng.wait_ge(x, y)
                    else:
                        a, kw = y
                        instr = getattr(eng, x)(*a, **kw)
                        for (h, k) in tok.incs:
                            instr.then_inc(h, k)
            getattr(self.block, e)(replay)
            ops.clear()


class Sem:
    def __init__(self, stack, nc, name):
        self.h = stack.enter_context(nc.semaphore(name))
        self.n = 0

    def inc(self, tok, k=1):
        tok.then_inc(self.h, k)
        self.n += k
        return self.n


_WMAX = {}


def _w(eng, sem, n=None):
    n = sem.n if n is None else n
    if n <= 0:
        return
    ename = getattr(eng, "_wname", None) or type(eng).__name__
    key = (ename, id(sem))
    if _WMAX.get(key, -1) >= n:
        return
    _WMAX[key] = n
    eng.wait_ge(sem.h, n)


STOP = "ALL"
SUB = 9
EVAC = "both"
_ORD = {"NONE": -1, "PRE": 0, "MID": 1, "RET": 2, "BCK1": 3, "BCK2": 4, "A": 5, "ALL": 6}


def _on(stage):
    return _ORD[stage] <= _ORD[STOP]


def build_core(nt=NT):
    _WMAX.clear()
    nc = bass.Bass("TRN2", target_bir_lowering=False)

    x = nc.dram_tensor("x", [T, C], F32, kind="ExternalInput").ap()
    wqT = nc.dram_tensor("wqT", [C, C], F32, kind="ExternalInput").ap()
    wkT = nc.dram_tensor("wkT", [C, C], F32, kind="ExternalInput").ap()
    wvT = nc.dram_tensor("wvT", [C, C], F32, kind="ExternalInput").ap()
    woT = nc.dram_tensor("woT", [C, C], F32, kind="ExternalInput").ap()
    ewT = nc.dram_tensor("ewT", [E, C, C], F32, kind="ExternalInput")
    cb16_d = nc.dram_tensor("cb16", [128, WB], BF16, kind="ExternalInput").ap()
    cf32_d = nc.dram_tensor("cf32", [128, WF], F32, kind="ExternalInput").ap()
    sel2_d = nc.dram_tensor("sel2", [2, 128], BF16, kind="ExternalInput").ap()

    out = nc.dram_tensor("out", [T, C], F32, kind="ExternalOutput").ap()
    xnd = nc.dram_tensor("xnd", [T, C], BF16).ap()

    st = ExitStack()
    with st:
        dx = Sem(st, nc, "dx")
        dx2 = Sem(st, nc, "dx2")
        dw = Sem(st, nc, "dw")
        dwg = Sem(st, nc, "dwg")
        dwk = Sem(st, nc, "dwk")
        dwv = Sem(st, nc, "dwv")
        dwo = Sem(st, nc, "dwo")
        dxn = Sem(st, nc, "dxn")
        dop = Sem(st, nc, "dop")
        dwe = Sem(st, nc, "dwe")
        dxe = Sem(st, nc, "dxe")
        pe = Sem(st, nc, "pe")
        dv = Sem(st, nc, "dv")
        ac = Sem(st, nc, "ac")
        pl = Sem(st, nc, "pl")

        sMask = st.enter_context(nc.sbuf_tensor("sMask", [128, NT * E], F32))

        B = [st.enter_context(nc.psum_tensor(f"ps{i}", [128, 512], F32))
             for i in range(6)]
        PBQ = st.enter_context(nc.psum_tensor("pbq", [128, 1024], BF16))
        PBK = st.enter_context(nc.psum_tensor("pbk", [128, 1024], BF16))

        block = st.enter_context(nc.Block())

        sWq = st.enter_context(nc.sbuf_tensor("sWq", [128, CH * C], BF16))
        sWk = st.enter_context(nc.sbuf_tensor("sWk", [128, CH * C], BF16))
        sWv = st.enter_context(nc.sbuf_tensor("sWv", [128, CH * C], BF16))

        stA = ExitStack()
        with stA:
            A = stA.enter_context
            sWo = A(nc.sbuf_tensor("sWo", [128, CH * C], BF16))
            sGnbB = A(nc.sbuf_tensor("sGnbB", [128, CH], BF16))
            sBeta = A(nc.sbuf_tensor("sBeta", [1, C], BF16))
            sOnes1 = A(nc.sbuf_tensor("sOnes1", [1, 128], BF16))
            sSel2 = A(nc.sbuf_tensor("sSel2", [2, 128], BF16))
            sBlk2 = A(nc.sbuf_tensor("sBlk2", [128, 2], BF16))
            sCB = A(nc.sbuf_tensor("sCB", [128, WB], BF16))
            sCF = A(nc.sbuf_tensor("sCF", [128, WF], F32))
            sX = A(nc.sbuf_tensor("sX", [128, 2 * C], F32))
            sX2 = A(nc.sbuf_tensor("sX2", [128, 2 * C], F32))
            sLn = A(nc.sbuf_tensor("sLn", [128, 10 * 12], F32))
            sXn = A(nc.sbuf_tensor("sXn", [128, 2 * C], F32))
            sXnF = A(nc.sbuf_tensor("sXnF", [128, 2 * C], F32))
            sXnB = A(nc.sbuf_tensor("sXnB", [128, 3 * C], BF16))
            sDump = A(nc.sbuf_tensor("sDump", [128, C], BF16))
            sSc = A(nc.sbuf_tensor("sSc", [8, 128], F32))
            sGsc = A(nc.sbuf_tensor("sGsc", [128, 32], F32))
            sQ = A(nc.sbuf_tensor("sQ", [128, 2 * C], BF16))
            sK = A(nc.sbuf_tensor("sK", [128, 2 * C], BF16))
            sRc = A(nc.sbuf_tensor("sRc", [128, C], BF16))
            sQt = A(nc.sbuf_tensor("sQt", [128, 2 * C], BF16))
            sKp = A(nc.sbuf_tensor("sKp", [128, 2 * C], BF16))
            sKtB = A(nc.sbuf_tensor("sKtB", [128, 3 * C], BF16))
            sV = A(nc.sbuf_tensor("sV", [128, 4 * C], BF16))
            sQf = A(nc.sbuf_tensor("sQf", [128, 2 * C], BF16))
            sKf = A(nc.sbuf_tensor("sKf", [128, 2 * C], BF16))
            sW1 = A(nc.sbuf_tensor("sW1", [128, 2 * 256], BF16))
            sS = A(nc.sbuf_tensor("sS", [128, HP * DH], F32))
            sSb = A(nc.sbuf_tensor("sSb", [128, HP * DH], BF16))
            sAv = A(nc.sbuf_tensor("sAv", [128, 2 * C], F32))
            sAvB = A(nc.sbuf_tensor("sAvB", [128, C], BF16))
            sAv2 = A(nc.sbuf_tensor("sAv2", [128, C], BF16))
            sSt = A(nc.sbuf_tensor("sSt", [2, 2048], F32))
            sStB = A(nc.sbuf_tensor("sStB", [2, 2048], BF16))
            sGn = A(nc.sbuf_tensor("sGn", [128, C], BF16))
            sOutp = A(nc.sbuf_tensor("sOutp", [128, 2 * C], F32))

            def _pre_sc(scalar):
                dw.inc(scalar.dma_start(out=sCF[:], in_=cf32_d), 16)
                dw.inc(scalar.dma_start(out=sCB[:], in_=cb16_d), 16)
                dw.inc(scalar.dma_start(out=sSel2[:], in_=sel2_d), 16)
            block.scalar(_pre_sc)

            def _pre_gp(gpsimd):
                dx.inc(gpsimd.dma_start(out=sX[:, 0:C], in_=x[0:128, :]), 16)
                dwg.inc(gpsimd.dma_start(
                    out=sWq[:].rearrange("p (j o) -> p j o", j=CH),
                    in_=wqT.rearrange("(j p) o -> p j o", p=128)), 16)
                dwk.inc(gpsimd.dma_start(
                    out=sWk[:].rearrange("p (j o) -> p j o", j=CH),
                    in_=wkT.rearrange("(j p) o -> p j o", p=128)), 16)
                dwv.inc(gpsimd.dma_start(
                    out=sWv[:].rearrange("p (j o) -> p j o", j=CH),
                    in_=wvT.rearrange("(j p) o -> p j o", p=128)), 16)
                dwo.inc(gpsimd.dma_start(
                    out=sWo[:].rearrange("p (j o) -> p j o", j=CH),
                    in_=woT.rearrange("(j p) o -> p j o", p=128)), 16)
                pl.inc(gpsimd.memset(sS[:], 0.0))
                pl.inc(gpsimd.memset(sSb[:], 0.0))
                pl.inc(gpsimd.memset(sOnes1[:], 1.0))
                gpsimd.memset(sBlk2[:], 0.0)
                gpsimd.memset(sBlk2[0:64, 0:1], 1.0 / 64.0)
                pl.inc(gpsimd.memset(sBlk2[64:128, 1:2], 1.0 / 64.0))
            block.gpsimd(_pre_gp)

            rec = Rec(block)
            guard = {}
            M = {}

            def need(eng, key):
                if key in guard:
                    s, n = guard[key]
                    _w(eng, s, n)

            pre_we = {}
            pre_xn = {}
            for it in range(nt + 9):
                tLN, tLN2, tXT, tGP = it, it - 1, it - 2, it - 3
                tRP, tRT, tRE, tG1, tG2 = it - 4, it - 5, it - 6, it - 7, it - 8

                def ok(t):
                    return 0 <= t < nt

                sy = rec.sync
                if tLN == 0:
                    M[("dx", 0)] = 16
                elif ok(tLN):
                    need(sy, ("sX", tLN % 2))
                    _w(sy, dx, dx.n)
                    dx.inc(sy.dma_start(
                        out=sX[:, (tLN % 2) * C:(tLN % 2) * C + C],
                        in_=x[tLN * 128:(tLN + 1) * 128, :]), 16)
                    M[("dx", tLN)] = dx.n
                if ok(tG1):
                    need(sy, ("sX2", tG1 % 2))
                    _w(sy, dx2, dx2.n)
                    dx2.inc(sy.dma_start(
                        out=sX2[:, (tG1 % 2) * C:(tG1 % 2) * C + C],
                        in_=x[tG1 * 128:(tG1 + 1) * 128, :]), 16)
                    M[("dx2", tG1)] = dx2.n
                if ok(tGP):
                    _w(sy, dv, M[("xnb", tGP)])
                    _w(sy, ac, M[("xnf", tGP)])
                    _w(sy, dxn, dxn.n)
                    dxn.inc(sy.dma_start(
                        out=xnd[tGP * 128:(tGP + 1) * 128, :],
                        in_=sXnB[:, (tGP % 3) * C:(tGP % 3) * C + C]), 16)
                    M[("dxn", tGP)] = dxn.n

                if ok(tXT):
                    t_ = tXT
                    xn_ = (t_ % 2) * C
                    te = rec.tensor
                    if t_ == 0:
                        _w(te, dw, 16)
                    _w(te, dv, M[("xn", t_)])
                    lq8 = min(it - 8, nt - 1)
                    if lq8 >= 0:
                        _w(te, dv, M[("w1", lq8, HP - 1)])
                    lo9 = min(it - 9, nt - 1)
                    if lo9 >= 0:
                        _w(te, dv, M[("outp", lo9)])
                    if t_ >= 1:
                        _w(te, ac, M[("xnf", t_ - 1)])
                        _w(te, dv, M[("xnb", t_ - 1)])
                    last = None
                    for j in range(CH):
                        b = B[j // 4]
                        last = te.transpose(b[:, (j % 4) * 128:(j % 4) * 128 + 128],
                                            sXn[:, xn_ + j * 128:xn_ + j * 128 + 128],
                                            sCF[:, 0:128])
                    pe.inc(last)
                    M[("xt", t_)] = pe.n
                    guard[("sXn", t_ % 2)] = (pe, pe.n)
                    sc3 = rec.scalar
                    xnf_ = (t_ % 2) * C
                    xnb_ = (t_ % 3) * C
                    _w(sc3, pe, M[("xt", t_)])
                    need(sc3, ("sXnF", t_ % 2))
                    need(sc3, ("sXnB", t_ % 3))
                    if t_ >= 3:
                        _w(sc3, dxn, M[("dxn", t_ - 3)])
                    sc3.copy(sXnF[:, xnf_:xnf_ + 512], B[0][:, 0:512])
                    ac.inc(sc3.copy(sXnB[:, xnb_:xnb_ + 512], B[0][:, 0:512]))
                    M[("xnf", t_)] = ac.n
                    ve3 = rec.vector
                    _w(ve3, pe, M[("xt", t_)])
                    need(ve3, ("sXnF", t_ % 2))
                    need(ve3, ("sXnB", t_ % 3))
                    if t_ >= 3:
                        _w(ve3, dxn, M[("dxn", t_ - 3)])
                    ve3.tensor_copy(sXnF[:, xnf_ + 512:xnf_ + 1024], B[1][:, 0:512])
                    dv.inc(ve3.tensor_copy(sXnB[:, xnb_ + 512:xnb_ + 1024],
                                           B[1][:, 0:512]))
                    M[("xnb", t_)] = dv.n

                if ok(tG2):
                    t_ = tG2
                    av_ = (t_ % 2) * C
                    te = rec.tensor
                    _w(te, dv, M[("stb", t_)])
                    if t_ == 0:
                        _w(te, dw, 48)
                    lr7 = min(it - 7, nt - 1)
                    if lr7 >= 0:
                        _w(te, dv, M[("w1", lr7, HP - 1)])
                    if ok(tG1) or tG1 >= nt:
                        lr = min(tG1, nt - 1)
                        _w(te, ac, M[("avev", lr)])
                        _w(te, dv, M[("supd", lr, HP - 1)])
                    _w(te, ac, M[("ste2", t_)])
                    te.matmul(B[2][:], sSel2[:], sStB[0:2, 0:512],
                              start=True, stop=True)
                    te.matmul(B[4][:], sSel2[:], sStB[0:2, 1024:1536],
                              start=True, stop=True)
                    te.matmul(B[3][:], sSel2[:], sStB[0:2, 512:1024],
                              start=True, stop=True)
                    pe.inc(te.matmul(B[5][:], sSel2[:], sStB[0:2, 1536:2048],
                                     start=True, stop=True))
                    M[("bcast", t_)] = pe.n
                    guard[("sStB", 0)] = (pe, pe.n)
                    ve = rec.vector
                    _w(ve, pe, M[("bcast", t_)])
                    need(ve, ("sGn", 0))
                    dv.inc(ve.tensor_sub(sAv[:, av_:av_ + 512], sAv[:, av_:av_ + 512],
                                         B[2][:]))
                    _w(ve, dv)
                    dv.inc(ve.tensor_tensor(sGn[:, 0:512], sAv[:, av_:av_ + 512],
                                            B[4][:], ALU.mult))
                    M[("gnh0", t_)] = dv.n
                    _w(ve, dv)
                    dv.inc(ve.tensor_sub(sAv[:, av_ + 512:av_ + 1024],
                                         sAv[:, av_ + 512:av_ + 1024], B[3][:]))
                    _w(ve, dv)
                    dv.inc(ve.tensor_tensor(sGn[:, 512:1024],
                                            sAv[:, av_ + 512:av_ + 1024],
                                            B[5][:], ALU.mult))
                    M[("gn", t_)] = dv.n
                    guard[("sAv", t_ % 2)] = (dv, dv.n)
                if ok(tRT):
                    t_ = tRT
                    qt_ = (t_ % 2) * C
                    te = rec.tensor
                    if t_ == 0:
                        _w(te, dw, 32)
                    _w(te, pl, M[("ropeq", t_)])
                    if t_ >= 1:
                        _w(te, ac, M[("qfev", t_ - 1)])
                        _w(te, dv, M[("kfev", t_ - 1)])
                    kp_ = (t_ % 2) * C
                    last = None
                    for hp in range(HP):
                        last = te.transpose(PBQ[:, hp * 128:(hp + 1) * 128],
                                            sQt[:, qt_ + hp * 128:qt_ + hp * 128 + 128],
                                            sCB[:, 2048:2176])
                    for hp in range(HP):
                        last = te.transpose(PBK[:, hp * 128:(hp + 1) * 128],
                                            sKp[:, kp_ + hp * 128:kp_ + hp * 128 + 128],
                                            sCB[:, 2048:2176])
                    pe.inc(last)
                    M[("ropet", t_)] = pe.n
                    guard[("sQt", t_ % 2)] = (pe, pe.n)
                    guard[("sKp", t_ % 2)] = (pe, pe.n)
                    sc_ = rec.scalar
                    qf_ = (t_ % 2) * C
                    _w(sc_, pe, M[("ropet", t_)])
                    need(sc_, ("sQf", t_ % 2))
                    sc_.copy(sQf[:, qf_:qf_ + 512], PBQ[:, 0:512])
                    ac.inc(sc_.copy(sQf[:, qf_ + 512:qf_ + 1024], PBQ[:, 512:1024]))
                    M[("qfev", t_)] = ac.n
                    ve3 = rec.vector
                    kf_ = (t_ % 2) * C
                    _w(ve3, pe, M[("ropet", t_)])
                    need(ve3, ("sKf", t_ % 2))
                    ve3.tensor_copy(sKf[:, kf_:kf_ + 512], PBK[:, 0:512])
                    dv.inc(ve3.tensor_copy(sKf[:, kf_ + 512:kf_ + 1024],
                                           PBK[:, 512:1024]))
                    M[("kfev", t_)] = dv.n

                if ok(tG2):
                    t_ = tG2
                    te2 = rec.tensor
                    _w(te2, dv, M[("gnh0", t_)])
                    if t_ == 0:
                        _w(te2, dv, M["wo_ready"])
                        _w(te2, pl, 3)
                    if it >= 1:
                        lq = min(it - 7, nt - 1)
                        if lq >= 0:
                            _w(te2, dv, M[("w1", lq, HP - 1)])
                        lx = min(it - 2, nt - 1)
                        if lx >= 0:
                            _w(te2, ac, M[("xnf", lx)])
                            _w(te2, dv, M[("xnb", lx)])
                    for hp in range(HP // 2):
                        te2.matmul(B[0][:], sGn[:, hp * 128:hp * 128 + 128],
                                   sWo[:, hp * C: hp * C + 512],
                                   start=(hp == 0), stop=False)
                        te2.matmul(B[1][:], sGn[:, hp * 128:hp * 128 + 128],
                                   sWo[:, hp * C + 512: hp * C + 1024],
                                   start=(hp == 0), stop=False)
                    _w(te2, dv, M[("gn", t_)])
                    for hp in range(HP // 2, HP):
                        te2.matmul(B[0][:], sGn[:, hp * 128:hp * 128 + 128],
                                   sWo[:, hp * C: hp * C + 512],
                                   start=False, stop=False)
                        te2.matmul(B[1][:], sGn[:, hp * 128:hp * 128 + 128],
                                   sWo[:, hp * C + 512: hp * C + 1024],
                                   start=False, stop=False)
                    te2.matmul(B[0][:], sOnes1[0:1, :], sBeta[0:1, 0:512],
                               start=False, stop=True)
                    pe.inc(te2.matmul(B[1][:], sOnes1[0:1, :], sBeta[0:1, 512:1024],
                                      start=False, stop=True))
                    M[("attn", t_)] = pe.n
                    guard[("sGn", 0)] = (pe, pe.n)
                    ve2 = rec.vector
                    x2_ = (t_ % 2) * C
                    ln_ = (t_ % 10) * 12
                    _w(ve2, pe, M[("attn", t_)])
                    _w(ve2, dx2, M[("dx2", t_)])
                    if t_ >= 2:
                        _w(ve2, dop, M[("dop", t_ - 2)])
                    ve2.scalar_tensor_tensor(
                        sOutp[:, x2_:x2_ + 512], sX2[:, x2_:x2_ + 512],
                        sLn[:, ln_ + 8:ln_ + 9], B[0][:], ALU.mult, ALU.add)
                    dv.inc(ve2.scalar_tensor_tensor(
                        sOutp[:, x2_ + 512:x2_ + 1024], sX2[:, x2_ + 512:x2_ + 1024],
                        sLn[:, ln_ + 8:ln_ + 9], B[1][:], ALU.mult, ALU.add))
                    _w(ve2, dv)
                    ve2.tensor_scalar_add(sOutp[:, x2_:x2_ + 512],
                                          sOutp[:, x2_:x2_ + 512],
                                          sLn[:, ln_ + 9:ln_ + 10])
                    dv.inc(ve2.tensor_scalar_add(sOutp[:, x2_ + 512:x2_ + 1024],
                                                 sOutp[:, x2_ + 512:x2_ + 1024],
                                                 sLn[:, ln_ + 9:ln_ + 10]))
                    M[("outp", t_)] = dv.n
                    guard[("sX2", t_ % 2)] = (dv, dv.n)
                    guard[("sLn", t_ % 10)] = (dv, dv.n)
                    sy2 = rec.sync
                    _w(sy2, dv, M[("outp", t_)])
                    _w(sy2, dop, dop.n)
                    dop.inc(sy2.dma_start(out=out[t_ * 128:(t_ + 1) * 128, :],
                                          in_=sOutp[:, x2_:x2_ + C]), 16)
                    M[("dop", t_)] = dop.n

                if ok(tG1):
                    t_ = tG1
                    a2_ = 0
                    st_ = 0
                    te = rec.tensor
                    if t_ == 0:
                        _w(te, pl, 4)
                    _w(te, ac, M[("av2", t_)])
                    _w(te, dv, M[("avb", t_)])
                    _w(te, dv, M[("w1", min(t_, nt - 1), HP - 1)])
                    if ok(tG2):
                        _w(te, dv, M[("gn", tG2)])
                    te.matmul(B[2][0:2, :], sBlk2[:], sAvB[:, a2_:a2_ + 512],
                              start=True, stop=True)
                    pe.inc(te.matmul(B[3][0:2, :], sBlk2[:], sAv2[:, a2_:a2_ + 512],
                                     start=True, stop=True))
                    M[("st1", t_)] = pe.n
                    sc2 = rec.scalar
                    _w(sc2, pe, M[("st1", t_)])
                    need(sc2, ("sSt", 0))
                    sc2.copy(sSt[0:2, st_:st_ + 512], B[2][0:2, :])
                    ac.inc(sc2.copy(sSt[0:2, st_ + 1024:st_ + 1536], B[3][0:2, :]))
                    M[("ste1", t_)] = ac.n
                    te2 = rec.tensor
                    _w(te2, ac, M[("ste1", t_)])
                    te2.matmul(B[2][0:2, :], sBlk2[:], sAvB[:, a2_ + 512:a2_ + 1024],
                               start=True, stop=True)
                    pe.inc(te2.matmul(B[3][0:2, :], sBlk2[:],
                                      sAv2[:, a2_ + 512:a2_ + 1024],
                                      start=True, stop=True))
                    M[("st2", t_)] = pe.n
                    guard[("sAvB", 0)] = (pe, pe.n)
                    guard[("sAv2", 0)] = (pe, pe.n)
                    sc3 = rec.scalar
                    _w(sc3, pe, M[("st2", t_)])
                    sc3.copy(sSt[0:2, st_ + 512:st_ + 1024], B[2][0:2, :])
                    ac.inc(sc3.copy(sSt[0:2, st_ + 1536:st_ + 2048], B[3][0:2, :]))
                    M[("ste2", t_)] = ac.n
                    ve8 = rec.vector
                    _w(ve8, ac, M[("ste2", t_)])
                    need(ve8, ("sStB", 0))
                    dv.inc(ve8.tensor_copy(sStB[0:2, st_:st_ + 1024],
                                           sSt[0:2, st_:st_ + 1024]))
                    _w(ve8, dv)
                    dv.inc(ve8.tensor_mul(sSt[0:2, st_:st_ + 1024],
                                          sSt[0:2, st_:st_ + 1024],
                                          sSt[0:2, st_:st_ + 1024]))
                    _w(ve8, dv)
                    dv.inc(ve8.tensor_sub(sSt[0:2, st_ + 1024:st_ + 2048],
                                          sSt[0:2, st_ + 1024:st_ + 2048],
                                          sSt[0:2, st_:st_ + 1024]))
                    M[("var", t_)] = dv.n

                if ok(tGP):
                    t_ = tGP
                    xnf_ = (t_ % 2) * C
                    te = rec.tensor
                    if t_ == 0:
                        _w(te, dw, 16)
                    _w(te, ac, M[("xnf", t_)])
                    _w(te, dv, M[("xnb", t_)])
                    if ok(tG2):
                        _w(te, dv, M[("gn", tG2)])
                    lre = min(it - 7, nt - 1)
                    if lre >= 0:
                        _w(te, ac, M[("avev", lre)])
                    if t_ >= 1:
                        _w(te, dv, M[("vev", t_ - 1)])
                        _w(te, ac, M[("qev", t_ - 1)])
                    last = None
                    for j in range(CH):
                        last = te.matmul(B[4][0:8, 0:128],
                                         sCF[:, 162 + j * E:162 + (j + 1) * E],
                                         sXnF[:, xnf_ + j * 128:xnf_ + j * 128 + 128],
                                         start=(j == 0), stop=(j == CH - 1))
                    pe.inc(last)
                    M[("gate", t_)] = pe.n
                    guard[("sXnF", t_ % 2)] = (pe, pe.n)
                    sc_ = rec.scalar
                    _w(sc_, pe, M[("gate", t_)])
                    need(sc_, ("sSc", 0))
                    ac.inc(sc_.copy(sSc[:, 0:128], B[4][0:8, 0:128]))
                    M[("scev", t_)] = ac.n
                    te1b = rec.tensor
                    _w(te1b, ac, M[("scev", t_)])
                    pe.inc(te1b.transpose(B[4][:, 128:136], sSc[:, 0:128],
                                          sCF[0:8, 0:8]))
                    M[("scT", t_)] = pe.n
                    guard[("sSc", 0)] = (pe, pe.n)
                    ve = rec.vector
                    g_ = 0
                    if t_ == 0:
                        _w(ve, dw, 16)
                    _w(ve, pe, M[("scT", t_)])
                    need(ve, ("sGsc", 0))
                    dv.inc(ve.tensor_copy(sGsc[:, g_:g_ + 8], B[4][:, 128:136]))
                    M[("maskcp", t_)] = dv.n
                    _w(ve, dv)
                    dv.inc(ve.reduce_max(sGsc[:, g_ + 8:g_ + 9], sGsc[:, g_:g_ + 8],
                                         axis=AX.X))
                    _w(ve, dv)
                    dv.inc(ve.tensor_single_scalar(sGsc[:, g_ + 16:g_ + 24],
                                                   sGsc[:, g_:g_ + 8],
                                                   sGsc[:, g_ + 8:g_ + 9],
                                                   ALU.is_equal))
                    _w(ve, dv)
                    dv.inc(ve.scalar_tensor_tensor(sGsc[:, g_ + 24:g_ + 32],
                                                   sGsc[:, g_ + 16:g_ + 24],
                                                   -1000.0, sCF[:, 136:144],
                                                   ALU.mult, ALU.add))
                    _w(ve, dv)
                    dv.inc(ve.tensor_reduce(sGsc[:, g_ + 9:g_ + 10],
                                            sGsc[:, g_ + 24:g_ + 32],
                                            axis=AX.X, op=ALU.min))
                    _w(ve, dv)
                    dv.inc(ve.tensor_single_scalar(
                        sMask[:, t_ * E:(t_ + 1) * E], sGsc[:, g_ + 24:g_ + 32],
                        sGsc[:, g_ + 9:g_ + 10], ALU.is_equal))
                    M[("mask", t_)] = dv.n
                    guard[("sGsc", 0)] = (dv, dv.n)

                if ok(tLN):
                    t_ = tLN
                    x_ = (t_ % 2) * C
                    ln_ = (t_ % 10) * 12
                    sc_ = rec.scalar
                    _w(sc_, dx, M[("dx", t_)])
                    need(sc_, ("sLn", t_ % 10))
                    _w(sc_, ac, ac.n)
                    ac.inc(sc_.activation(sDump[:], sX[:, x_:x_ + C], AF.Square,
                                          accum_out=sLn[:, ln_ + 1:ln_ + 2]))
                    M[("lnsq", t_)] = ac.n
                    ve = rec.vector
                    _w(ve, dx, M[("dx", t_)])
                    _w(ve, ac, M[("lnsq", t_)])
                    need(ve, ("sLn", t_ % 10))
                    dv.inc(ve.reduce_sum(sLn[:, ln_:ln_ + 1], sX[:, x_:x_ + C],
                                         axis=AX.X))
                    _w(ve, dv)
                    ve.tensor_scalar_mul(sLn[:, ln_ + 2:ln_ + 3],
                                         sLn[:, ln_:ln_ + 1], 1.0 / C)
                    dv.inc(ve.tensor_scalar_mul(sLn[:, ln_ + 3:ln_ + 4],
                                                sLn[:, ln_ + 1:ln_ + 2], 1.0 / C))
                    _w(ve, dv)
                    dv.inc(ve.tensor_mul(sLn[:, ln_ + 4:ln_ + 5],
                                         sLn[:, ln_ + 2:ln_ + 3],
                                         sLn[:, ln_ + 2:ln_ + 3]))
                    _w(ve, dv)
                    dv.inc(ve.tensor_sub(sLn[:, ln_ + 5:ln_ + 6],
                                         sLn[:, ln_ + 3:ln_ + 4],
                                         sLn[:, ln_ + 4:ln_ + 5]))
                    M[("lnvar", t_)] = dv.n

                if ok(tLN2):
                    t_ = tLN2
                    x_ = (t_ % 2) * C
                    ln_ = (t_ % 10) * 12
                    sc2 = rec.scalar
                    if t_ == 0:
                        _w(sc2, dw, 16)
                    _w(sc2, dv, M[("lnvar", t_)])
                    ac.inc(sc2.activation(sLn[:, ln_ + 6:ln_ + 7],
                                          sLn[:, ln_ + 5:ln_ + 6],
                                          AF.Ln, bias=sCF[:, 144:145]))
                    _w(sc2, ac)
                    ac.inc(sc2.activation(sLn[:, ln_ + 7:ln_ + 8],
                                          sLn[:, ln_ + 6:ln_ + 7], AF.Exp,
                                          scale=sCF[:, 145:146]))
                    M[("lnr", t_)] = ac.n
                    ve2 = rec.vector
                    _w(ve2, ac, M[("lnr", t_)])
                    ve2.tensor_scalar_add(sLn[:, ln_ + 8:ln_ + 9],
                                          sLn[:, ln_ + 7:ln_ + 8], 1.0)
                    dv.inc(ve2.scalar_tensor_tensor(
                        sLn[:, ln_ + 9:ln_ + 10], sLn[:, ln_ + 2:ln_ + 3], -1.0,
                        sLn[:, ln_ + 7:ln_ + 8], ALU.mult, ALU.mult))
                    need(ve2, ("sXn", t_ % 2))
                    _w(ve2, dv)
                    xn_ = (t_ % 2) * C
                    dv.inc(ve2.tensor_scalar(sXn[:, xn_:xn_ + C], sX[:, x_:x_ + C],
                                             sLn[:, ln_ + 2:ln_ + 3],
                                             sLn[:, ln_ + 7:ln_ + 8],
                                             ALU.subtract, ALU.mult))
                    M[("xn", t_)] = dv.n
                    guard[("sX", t_ % 2)] = (dv, dv.n)

                if ok(tG1):
                    t_ = tG1
                    st_ = 0
                    sc4 = rec.scalar
                    _w(sc4, dv, M[("var", t_)])
                    ac.inc(sc4.activation(sSt[0:2, st_ + 1024:st_ + 2048],
                                          sSt[0:2, st_ + 1024:st_ + 2048], AF.Ln,
                                          bias=sCF[0:2, 144:145]))
                    _w(sc4, ac)
                    ac.inc(sc4.activation(sSt[0:2, st_ + 1024:st_ + 2048],
                                          sSt[0:2, st_ + 1024:st_ + 2048], AF.Exp,
                                          scale=sCF[0:2, 145:146]))
                    M[("rstd", t_)] = ac.n
                    ve9 = rec.vector
                    _w(ve9, ac, M[("rstd", t_)])
                    dv.inc(ve9.tensor_copy(sStB[0:2, st_ + 1024:st_ + 2048],
                                           sSt[0:2, st_ + 1024:st_ + 2048]))
                    M[("stb", t_)] = dv.n
                    guard[("sSt", 0)] = (dv, dv.n)

                if ok(tGP):
                    t_ = tGP
                    xnf_ = (t_ % 2) * C
                    xnb_ = (t_ % 3) * C
                    teq1 = rec.tensor
                    if t_ == 0:
                        _w(teq1, dwg, 16)
                    _w(teq1, dv, M[("xnb", t_)])
                    _w(teq1, ac, M[("xnf", t_)])
                    if ok(tG2):
                        _w(teq1, dv, M[("gn", tG2)])
                    lastq = None
                    for j in range(CH):
                        lhs = sXnB[:, xnb_ + j * 128:xnb_ + j * 128 + 128]
                        lastq = teq1.matmul(B[5][:], lhs,
                                            sWq[:, j * C + 512:j * C + 1024],
                                            start=(j == 0), stop=(j == CH - 1))
                    pe.inc(lastq)
                    M[("projq1", t_)] = pe.n
                    scq1 = rec.scalar
                    q_ = (t_ % 2) * C
                    _w(scq1, pe, M[("projq1", t_)])
                    need(scq1, ("sQ", t_ % 2))
                    ac.inc(scq1.copy(sQ[:, q_ + 512:q_ + 1024], B[5][:]))
                    M[("qev1", t_)] = ac.n
                    te3 = rec.tensor
                    if t_ == 0:
                        _w(te3, dwk, 16)
                    if t_ == 3:
                        _w(te3, dv, M["beta_evac"])
                    if ok(tG1):
                        _w(te3, ac, M[("ste2", tG1)])
                    elif t_ >= 1:
                        _w(te3, ac, M[("qev", t_ - 1)])
                    lastk = None
                    for j in range(CH):
                        lhs = sXnB[:, xnb_ + j * 128:xnb_ + j * 128 + 128]
                        lastk = te3.matmul(B[2][:], lhs, sWk[:, j * C:j * C + 512],
                                           start=(j == 0), stop=(j == CH - 1))
                        lastk = te3.matmul(B[3][:], lhs,
                                           sWk[:, j * C + 512:j * C + 1024],
                                           start=(j == 0), stop=(j == CH - 1))
                    pe.inc(lastk)
                    M[("projk", t_)] = pe.n
                    sc3 = rec.scalar
                    k_ = (t_ % 2) * C
                    _w(sc3, pe, M[("projk", t_)])
                    need(sc3, ("sK", t_ % 2))
                    sc3.copy(sK[:, k_:k_ + 512], B[2][:])
                    ac.inc(sc3.copy(sK[:, k_ + 512:k_ + 1024], B[3][:]))
                    M[("kev", t_)] = ac.n
                    te4 = rec.tensor
                    if t_ == 0:
                        _w(te4, dwv, 16)
                    _w(te4, dv, M[("maskcp", t_)])
                    _w(te4, ac, M[("qev1", t_)])
                    lastv = None
                    for j in range(CH):
                        lhs = sXnB[:, xnb_ + j * 128:xnb_ + j * 128 + 128]
                        lastv = te4.matmul(B[4][:], lhs, sWv[:, j * C:j * C + 512],
                                           start=(j == 0), stop=(j == CH - 1))
                        lastv = te4.matmul(B[5][:], lhs,
                                           sWv[:, j * C + 512:j * C + 1024],
                                           start=(j == 0), stop=(j == CH - 1))
                    pe.inc(lastv)
                    M[("projv", t_)] = pe.n
                    ve0 = rec.vector
                    v_ = (t_ % 4) * C
                    _w(ve0, pe, M[("projv", t_)])
                    need(ve0, ("sV", t_ % 4))
                    ve0.tensor_copy(sV[:, v_:v_ + 512], B[4][:])
                    dv.inc(ve0.tensor_copy(sV[:, v_ + 512:v_ + 1024], B[5][:]))
                    M[("vev", t_)] = dv.n
                    teq0 = rec.tensor
                    _w(teq0, ac, M[("kev", t_)])
                    lastq = None
                    for j in range(CH):
                        lhs = sXnB[:, xnb_ + j * 128:xnb_ + j * 128 + 128]
                        lastq = teq0.matmul(B[2][:], lhs, sWq[:, j * C:j * C + 512],
                                            start=(j == 0), stop=(j == CH - 1))
                    pe.inc(lastq)
                    M[("projq", t_)] = pe.n
                    guard[("sXnB", t_ % 3)] = (pe, pe.n)
                    scq0 = rec.scalar
                    _w(scq0, pe, M[("projq", t_)])
                    ac.inc(scq0.copy(sQ[:, q_:q_ + 512], B[2][:]))
                    M[("qev", t_)] = ac.n

                if ok(tRP):
                    t_ = tRP
                    q_ = (t_ % 2) * C
                    k_ = (t_ % 2) * C
                    qt_ = (t_ % 2) * C
                    kt2_ = (t_ % 3) * C
                    co = t_ * DH
                    cosb = bass.AP(sCB, 2176 + co, [[WB, 128], [0, H], [1, DH]])
                    ssb0 = bass.AP(sCB, 4224 + co, [[WB, 128], [0, H], [1, 32]])
                    ssb1 = bass.AP(sCB, 4256 + co, [[WB, 128], [0, H], [1, 32]])
                    rr0 = bass.AP(sRc, 0, [[C, 128], [DH, H], [1, 32]])
                    rr1 = bass.AP(sRc, 32, [[C, 128], [DH, H], [1, 32]])
                    po2 = rec.gpsimd
                    if t_ == 0:
                        _w(po2, dw, 32)
                    _w(po2, ac, M[("kev", t_)])
                    kp_ = (t_ % 2) * C
                    need(po2, ("sKp", t_ % 2))
                    kc3 = bass.AP(sKp, kp_, [[2 * C, 128], [DH, H], [1, DH]])
                    k3 = bass.AP(sK, k_, [[2 * C, 128], [DH, H], [1, DH]])
                    po2.tensor_tensor(kc3, k3, cosb, ALU.mult)
                    kh0 = bass.AP(sK, k_, [[2 * C, 128], [DH, H], [1, 32]])
                    kh1 = bass.AP(sK, k_ + 32, [[2 * C, 128], [DH, H], [1, 32]])
                    po2.tensor_tensor(rr0, kh1, ssb0, ALU.mult)
                    pl.inc(po2.tensor_tensor(rr1, kh0, ssb1, ALU.mult))
                    pl.inc(po2.tensor_add(sKp[:, kp_:kp_ + C],
                                          sKp[:, kp_:kp_ + C], sRc[:]))
                    need(po2, ("sKtB", t_ % 3))
                    pl.inc(po2.tensor_mul(sKtB[:, kt2_:kt2_ + C],
                                          sKp[:, kp_:kp_ + C],
                                          sCB[:, 1024:2048]))
                    M[("ropek", t_)] = pl.n
                    guard[("sK", t_ % 2)] = (pl, pl.n)
                    guard[("sKp", t_ % 2)] = (pl, pl.n)
                    po = rec.gpsimd
                    _w(po, ac, M[("qev", t_)])
                    _w(po, ac, M[("qev1", t_)])
                    _w(po, pl, M[("ropek", t_)])
                    need(po, ("sQt", t_ % 2))
                    qc3 = bass.AP(sQt, qt_, [[2 * C, 128], [DH, H], [1, DH]])
                    q3 = bass.AP(sQ, q_, [[2 * C, 128], [DH, H], [1, DH]])
                    po.tensor_tensor(qc3, q3, cosb, ALU.mult)
                    qh0 = bass.AP(sQ, q_, [[2 * C, 128], [DH, H], [1, 32]])
                    qh1 = bass.AP(sQ, q_ + 32, [[2 * C, 128], [DH, H], [1, 32]])
                    po.tensor_tensor(rr0, qh1, ssb0, ALU.mult)
                    pl.inc(po.tensor_tensor(rr1, qh0, ssb1, ALU.mult))
                    pl.inc(po.tensor_add(sQt[:, qt_:qt_ + C],
                                         sQt[:, qt_:qt_ + C], sRc[:]))
                    pl.inc(po.tensor_mul(sQt[:, qt_:qt_ + C],
                                         sQt[:, qt_:qt_ + C],
                                         sCB[:, 0:1024]))
                    M[("ropeq", t_)] = pl.n
                    guard[("sQ", t_ % 2)] = (pl, pl.n)

                if ok(tRE):
                    t_ = tRE
                    qf_ = (t_ % 2) * C
                    kf_ = (t_ % 2) * C
                    v_ = (t_ % 4) * C
                    kt_ = (t_ % 3) * C
                    av_ = (t_ % 2) * C

                    def emit_qk(hp):
                        qb = [B[0], B[1]] if hp % 2 == 0 else [B[2], B[3]]
                        te = rec.tensor
                        if hp == 0:
                            _w(te, ac, M[("qfev", t_)])
                            _w(te, dv, M[("kfev", t_)])
                            lxt = min(tXT, nt - 1)
                            _w(te, ac, M[("xnf", lxt)])
                            _w(te, dv, M[("xnb", lxt)])
                            if ok(tG2):
                                _w(te, dv, M[("outp", tG2)])
                        if hp == 1:
                            _w(te, ac, M[("qev", min(tGP, nt - 1))])
                            if ok(tG1):
                                _w(te, ac, M[("ste2", tG1)])
                            if ok(tG2):
                                _w(te, dv, M[("gn", tG2)])
                        if hp >= 2:
                            _w(te, dv, M[("w1", t_, hp - 2)])
                        te.matmul(qb[0][:, 0:128],
                                  sKf[0:64, kf_ + hp * 128:kf_ + hp * 128 + 128],
                                  sQf[0:64, qf_ + hp * 128:qf_ + hp * 128 + 128],
                                  start=True, stop=True, tile_position=(0, 0))
                        pe.inc(te.matmul(
                            qb[1][:, 0:128],
                            sKf[64:128, kf_ + hp * 128:kf_ + hp * 128 + 128],
                            sQf[64:128, qf_ + hp * 128:qf_ + hp * 128 + 128],
                            start=True, stop=True, tile_position=(64, 0)))
                        M[("qk", t_, hp)] = pe.n

                    def emit_w1(hp):
                        hA, hB = 2 * hp, 2 * hp + 1
                        w_ = (hp % 2) * 256
                        qb = [B[0], B[1]] if hp % 2 == 0 else [B[2], B[3]]
                        ve = rec.vector
                        if t_ == 0 and hp == 0:
                            _w(ve, dw, 32)
                        _w(ve, pe, M[("qk", t_, hp)])
                        ve.tensor_tensor(sW1[:, w_:w_ + 128], qb[0][:, 0:128],
                                         sCB[:, 6272 + hA * 128:6400 + hA * 128],
                                         ALU.mult)
                        dv.inc(ve.tensor_tensor(sW1[:, w_ + 128:w_ + 256],
                                                qb[1][:, 0:128],
                                                sCB[:, 6272 + hB * 128:
                                                    6400 + hB * 128],
                                                ALU.mult))
                        M[("w1", t_, hp)] = dv.n

                    def emit_av(hp):
                        hA, hB = 2 * hp, 2 * hp + 1
                        w_ = (hp % 2) * 256
                        te2 = rec.tensor
                        _w(te2, dv, M[("w1", t_, hp)])
                        if hp == 0:
                            _w(te2, dv, M[("vev", min(tGP, nt - 1))])
                        if hp == 0 and t_ >= 1:
                            _w(te2, ac, M[("avev", t_ - 1)])
                        if hp > 0:
                            _w(te2, ac, M[("avev1", t_, hp - 1)])
                        te2.matmul(B[4][0:64, 0:128],
                                   sV[:, v_ + hA * 64:v_ + hA * 64 + 64],
                                   sW1[:, w_:w_ + 128], start=True,
                                   stop=(t_ == 0), tile_position=(0, 0),
                                   skip_group_check=True)
                        te2.matmul(B[4][64:128, 0:128],
                                   sV[:, v_ + hB * 64:v_ + hB * 64 + 64],
                                   sW1[:, w_ + 128:w_ + 256], start=True,
                                   stop=(t_ == 0), tile_position=(0, 64),
                                   skip_group_check=True)
                        if t_ > 0:
                            _w(te2, ac, M[("sb", t_ - 1, hp)])
                            te2.matmul(B[4][0:64, 0:128],
                                       sSb[0:64, hp * 64:(hp + 1) * 64],
                                       sQf[0:64, qf_ + hp * 128:qf_ + hp * 128 + 128],
                                       start=False, stop=True, tile_position=(0, 0),
                                       skip_group_check=True)
                            te2.matmul(B[4][64:128, 0:128],
                                       sSb[64:128, hp * 64:(hp + 1) * 64],
                                       sQf[64:128, qf_ + hp * 128:qf_ + hp * 128 + 128],
                                       start=False, stop=True,
                                       tile_position=(64, 64),
                                       skip_group_check=True)
                        if hp > 1:
                            _w(te2, dv, M[("supd", t_, hp - 2)])
                        kvo = (hp % 2) * 64
                        te2.matmul(B[5][0:64, kvo:kvo + 64],
                                   sKtB[:, kt_ + hA * 64:kt_ + hA * 64 + 64],
                                   sV[:, v_ + hA * 64:v_ + hA * 64 + 64],
                                   start=True, stop=True, tile_position=(0, 0),
                                   skip_group_check=True)
                        pe.inc(te2.matmul(
                            B[5][64:128, kvo:kvo + 64],
                            sKtB[:, kt_ + hB * 64:kt_ + hB * 64 + 64],
                            sV[:, v_ + hB * 64:v_ + hB * 64 + 64],
                            start=True, stop=True, tile_position=(0, 64),
                            skip_group_check=True))
                        M[("avkv", t_, hp)] = pe.n
                        sc_ = rec.scalar
                        _w(sc_, pe, M[("avkv", t_, hp)])
                        if hp == 0:
                            need(sc_, ("sAv", t_ % 2))
                        ac.inc(sc_.copy(sAv[:, av_ + hp * 128:av_ + hp * 128 + 128],
                                        B[4][:, 0:128]))
                        M[("avev1", t_, hp)] = ac.n
                        ve2 = rec.vector
                        _w(ve2, pe, M[("avkv", t_, hp)])
                        if t_ == 0:
                            _w(ve2, pl, 2)
                        if t_ >= 1:
                            _w(ve2, ac, M[("sb", t_ - 1, hp)])
                        dv.inc(ve2.scalar_tensor_tensor(
                            sS[:, hp * 64:(hp + 1) * 64],
                            sS[:, hp * 64:(hp + 1) * 64],
                            sCF[:, 128 + hp:129 + hp], B[5][:, kvo:kvo + 64],
                            ALU.mult, ALU.add))
                        M[("supd", t_, hp)] = dv.n
                        sc2_ = rec.scalar
                        _w(sc2_, dv, M[("supd", t_, hp)])
                        ac.inc(sc2_.copy(sSb[:, hp * 64:(hp + 1) * 64],
                                         sS[:, hp * 64:(hp + 1) * 64]))
                        M[("sb", t_, hp)] = ac.n

                    emit_qk(0)
                    emit_w1(0)
                    emit_qk(1)
                    emit_w1(1)
                    for hp in range(HP):
                        emit_av(hp)
                        if hp + 2 < HP:
                            emit_qk(hp + 2)
                            emit_w1(hp + 2)
                    M[("avev", t_)] = M[("avev1", t_, HP - 1)]
                    M[("ret", t_)] = M[("avkv", t_, HP - 1)]
                    guard[("sQf", t_ % 2)] = (pe, M[("ret", t_)])
                    guard[("sKf", t_ % 2)] = (pe, M[("qk", t_, HP - 1)])
                    guard[("sV", t_ % 4)] = (pe, M[("ret", t_)])
                    guard[("sKtB", t_ % 3)] = (pe, M[("ret", t_)])

                if ok(tRE):
                    t_ = tRE
                    av_ = (t_ % 2) * C
                    sc_ = rec.scalar
                    _w(sc_, ac, M[("avev", t_)])
                    need(sc_, ("sAv2", 0))
                    ac.inc(sc_.activation(sAv2[:, 0:C], sAv[:, av_:av_ + C],
                                          AF.Square))
                    M[("av2", t_)] = ac.n
                    veA = rec.vector
                    need(veA, ("sAvB", 0))
                    _w(veA, ac, M[("avev", t_)])
                    dv.inc(veA.tensor_copy(sAvB[:, 0:C], sAv[:, av_:av_ + C]))
                    M[("avb", t_)] = dv.n

                if it == 0:
                    scg = rec.scalar
                    _w(scg, dw, 16)
                    ac.inc(scg.copy(sGnbB[:], sCF[:, 154:162]))
                    M["gnbB"] = ac.n

                if it == 5:
                    teb = rec.tensor
                    _w(teb, ac, M[("qev", 2)])
                    _w(teb, ac, M[("kev", 2)])
                    _w(teb, dwo, 16)
                    _w(teb, ac, M["gnbB"])
                    for j in range(CH):
                        teb.matmul(B[2][0:1, :], sGnbB[:, j:j + 1],
                                   sWo[:, j * C: j * C + 512],
                                   start=(j == 0), stop=(j == CH - 1))
                        last = teb.matmul(B[3][0:1, :], sGnbB[:, j:j + 1],
                                          sWo[:, j * C + 512: j * C + 1024],
                                          start=(j == 0), stop=(j == CH - 1))
                    pe.inc(last)
                    M["beta_pe"] = pe.n
                    veb = rec.vector
                    _w(veb, pe, M["beta_pe"])
                    _w(veb, dw, 16)
                    veb.tensor_copy(sBeta[0:1, 0:512], B[2][0:1, :])
                    dv.inc(veb.tensor_copy(sBeta[0:1, 512:1024], B[3][0:1, :]))
                    M["beta_evac"] = dv.n
                    last = None
                    for j in range(CH):
                        last = veb.tensor_scalar_mul(
                            sWo[:, j * C:(j + 1) * C], sWo[:, j * C:(j + 1) * C],
                            sCF[:, 146 + j:147 + j])
                    dv.inc(last)
                    M["wo_ready"] = dv.n

                if it == nt + 3 and _on("ALL"):
                    pog = rec.gpsimd
                    _w(pog, ac, M[("qev", nt - 1)])
                    _w(pog, dwe, dwe.n)
                    dwe.inc(pog.dma_start(
                        out=sWq[:].rearrange("p (j o) -> p j o", j=CH),
                        in_=bass.AP(ewT, 0,
                                    [[C, 128], [128 * C, CH], [1, C]])), 16)
                    pre_we[0] = dwe.n
                    _w(pog, ac, M[("kev", nt - 1)])
                    _w(pog, dwe, dwe.n)
                    dwe.inc(pog.dma_start(
                        out=sWk[:].rearrange("p (j o) -> p j o", j=CH),
                        in_=bass.AP(ewT, C * C,
                                    [[C, 128], [128 * C, CH], [1, C]])), 16)
                    pre_we[1] = dwe.n
                    scp = rec.scalar
                    _w(scp, dv, M[("vev", nt - 1)])
                    _w(scp, dxn, M[("dxn", 3)])
                    _w(scp, dxe, dxe.n)
                    dxe.inc(scp.dma_start(
                        out=sWv[:, 0:4 * C].rearrange(
                            "p (tl cc) -> p tl cc", tl=4),
                        in_=xnd[0:512, :].rearrange(
                            "(tl p) cc -> p tl cc", p=128)), 16)
                    pre_xn[0] = dxe.n
                    _w(scp, dxn, M[("dxn", 7)])
                    _w(scp, dxe, dxe.n)
                    dxe.inc(scp.dma_start(
                        out=sWv[:, 4 * C:8 * C].rearrange(
                            "p (tl cc) -> p tl cc", tl=4),
                        in_=xnd[512:1024, :].rearrange(
                            "(tl p) cc -> p tl cc", p=128)), 16)
                    pre_xn[1] = dxe.n

                rec.flush()

            a_pe, a_dv, a_ac, a_pl = pe.n, dv.n, ac.n, pl.n
            a_dop, a_dxn = dop.n, dxn.n

        if not _on("ALL"):
            def _fin(sync):
                _w(sync, pe, pe.n)
                _w(sync, dv, dv.n)
                _w(sync, ac, ac.n)
                _w(sync, pl, pl.n)
                _w(sync, dop, dop.n)
                _w(sync, dxn, dxn.n)
                dop.inc(sync.dma_start(out=out[0:128, :], in_=sOutp[:, 0:C]), 16)
                _w(sync, dop, dop.n)
            block.sync(_fin)
            return nc
        stB = ExitStack()
        with stB:
            sAcc = stB.enter_context(nc.sbuf_tensor("sAcc", [128, nt * C], F32))

            NG = (nt + 3) // 4

            acc_ld = {}

            def _b_acc(scalar):
                _w(scalar, pe, a_pe)
                _w(scalar, dv, a_dv)
                _w(scalar, pl, a_pl)
                for i in range(nt):
                    _w(scalar, dop, M[("dop", i)])
                    _w(scalar, dx, dx.n)
                    dx.inc(scalar.dma_start(out=sAcc[:, i * C:(i + 1) * C],
                                            in_=out[i * 128:(i + 1) * 128, :]),
                           16)
                    acc_ld[i] = dx.n
            block.scalar(_b_acc)

            mm_done = {}
            stt_done = {}
            xnld = {}

            for e in range(E):
                wslot = e % 2
                sWE = sWq if wslot == 0 else sWk

                if e < 2:
                    we_cnt = pre_we[e]
                else:
                    def _b_we(gpsimd, e=e, wslot=wslot, sWE=sWE):
                        _w(gpsimd, pe, a_pe)
                        _w(gpsimd, dv, a_dv)
                        _w(gpsimd, ac, a_ac)
                        _w(gpsimd, pl, a_pl)
                        _w(gpsimd, pe, mm_done[(e - 2, nt - 1)])
                        _w(gpsimd, dwe, dwe.n)
                        dwe.inc(gpsimd.dma_start(
                            out=sWE[:].rearrange("p (j o) -> p j o", j=CH),
                            in_=bass.AP(ewT, e * C * C,
                                        [[C, 128], [128 * C, CH], [1, C]])), 16)
                    block.gpsimd(_b_we)
                    we_cnt = dwe.n

                for g in range(NG):
                    s_flat = e * NG + g
                    gslot = s_flat % 2

                    if s_flat < 2:
                        xnld[(e, g)] = pre_xn[s_flat]
                    else:
                        def _b_xn(sync, e=e, g=g, gslot=gslot, s_flat=s_flat):
                            _w(sync, pe, a_pe)
                            _w(sync, dv, a_dv)
                            _w(sync, ac, a_ac)
                            _w(sync, pl, a_pl)
                            _w(sync, dxn, a_dxn)
                            pe_, pg_ = divmod(s_flat - 2, NG)
                            last_i = min(pg_ * 4 + 3, nt - 1)
                            _w(sync, pe, mm_done[(pe_, last_i)])
                            ntl = min(4, nt - g * 4)
                            _w(sync, dxe, dxe.n)
                            dxe.inc(sync.dma_start(
                                out=sWv[:, gslot * 4 * C: gslot * 4 * C + ntl * C
                                        ].rearrange("p (tl cc) -> p tl cc", tl=ntl),
                                in_=xnd[g * 512:g * 512 + ntl * 128, :].rearrange(
                                    "(tl p) cc -> p tl cc", p=128)), 16)
                        block.sync(_b_xn)
                        xnld[(e, g)] = dxe.n

                    for i in range(g * 4, min(g * 4 + 4, nt)):
                        flat = e * nt + i
                        bb = (flat % 3) * 2
                        xoff = (gslot * 4 + (i - g * 4)) * C

                        def _b_mm(tensor, e=e, i=i, bb=bb, xoff=xoff, flat=flat,
                                  we_cnt=we_cnt, g=g, sWE=sWE):
                            _w(tensor, dwe, we_cnt)
                            _w(tensor, dxe, xnld[(e, g)])
                            if flat < 3:
                                _w(tensor, dv, a_dv)
                                _w(tensor, ac, a_ac)
                                _w(tensor, pl, a_pl)
                            if flat >= 3:
                                key = divmod(flat - 3, nt)
                                _w(tensor, dv, stt_done[key])
                            last = None
                            for j in range(CH):
                                for half in range(2):
                                    last = tensor.matmul(
                                        B[bb + half][:],
                                        sWv[:, xoff + j * 128: xoff + j * 128 + 128],
                                        sWE[:, j * C + half * 512:
                                            j * C + half * 512 + 512],
                                        start=(j == 0), stop=(j == CH - 1))
                            pe.inc(last)
                        block.tensor(_b_mm)
                        mm_done[(e, i)] = pe.n

                        def _b_stt(vector, e=e, i=i, bb=bb):
                            _w(vector, pe, mm_done[(e, i)])
                            if e == 0:
                                _w(vector, dx, acc_ld[i])
                            else:
                                _w(vector, dv, stt_done[(e - 1, i)])
                            vector.scalar_tensor_tensor(
                                sAcc[:, i * C:i * C + 512],
                                B[bb][:], sMask[:, i * E + e:i * E + e + 1],
                                sAcc[:, i * C:i * C + 512], ALU.mult, ALU.add)
                            dv.inc(vector.scalar_tensor_tensor(
                                sAcc[:, i * C + 512:(i + 1) * C],
                                B[bb + 1][:], sMask[:, i * E + e:i * E + e + 1],
                                sAcc[:, i * C + 512:(i + 1) * C],
                                ALU.mult, ALU.add))
                        block.vector(_b_stt)
                        stt_done[(e, i)] = dv.n

            def _b_out(sync):
                for i in range(nt):
                    _w(sync, dv, stt_done[(E - 1, i)])
                    dop.inc(sync.dma_start(out=out[i * 128:(i + 1) * 128, :],
                                           in_=sAcc[:, i * C:(i + 1) * C]), 16)
                _w(sync, dop)
                _w(sync, dxn)
            block.sync(_b_out)

    return nc


def _prep_inputs(L0_attn_x, L0_attn_Wq, L0_attn_Wk, L0_attn_Wv, L0_attn_Wo,
                 L0_attn_gn_weight, L0_attn_gn_bias, L0_ffn_gate_w,
                 L0_ffn_expert_w):
    cst = _host_consts()
    f32 = np.float32
    gnw = np.asarray(L0_attn_gn_weight, f32).reshape(CH, 128).T
    gnb = np.asarray(L0_attn_gn_bias, f32).reshape(CH, 128).T
    gw = np.asarray(L0_ffn_gate_w, f32)
    gwp = gw.T.reshape(CH, 128, E).transpose(1, 0, 2).reshape(128, CH * E)
    cf32 = np.concatenate([cst["identf"], cst["gp"], cst["cvec"], cst["epsv"],
                           gnw, gnb, gwp], axis=1).astype(f32)
    assert cf32.shape == (128, WF), cf32.shape
    common = dict(
        wqT=np.ascontiguousarray(np.asarray(L0_attn_Wq, f32).T),
        wkT=np.ascontiguousarray(np.asarray(L0_attn_Wk, f32).T),
        wvT=np.ascontiguousarray(np.asarray(L0_attn_Wv, f32).T),
        woT=np.ascontiguousarray(np.asarray(L0_attn_Wo, f32).T),
        ewT=np.ascontiguousarray(np.asarray(L0_ffn_expert_w, f32).transpose(0, 2, 1)),
        cb16=cst["cb16"],
        cf32=np.ascontiguousarray(cf32),
        sel2=cst["sel2"])
    xs = np.asarray(L0_attn_x, f32)
    in_maps = [dict(common, x=np.ascontiguousarray(xs[b])) for b in range(8)]
    return in_maps


def kernel(**inputs):
    in_maps = _prep_inputs(**inputs)
    nc = build_core()
    res = run_bass_kernel_spmd(nc, in_maps, core_ids=list(range(8)))
    return np.stack([r["out"] for r in res.results], axis=0)
